# version 06
# baseline (speedup 1.0000x reference)
import math
from contextlib import ExitStack

import numpy as np

import concourse.bass as bass
import concourse.mybir as mybir
from concourse.bass_utils import run_bass_kernel_spmd

F32 = mybir.dt.float32
F16 = mybir.dt.float16
BF16 = mybir.dt.bfloat16
I32 = mybir.dt.int32
AF = mybir.ActivationFunctionType
ALU = mybir.AluOpType
AX = mybir.AxisListType

D = 2048
KC = 16
H = 16
FF = 8192
FNC = 64
LN_EPS = 1e-5
ATTN_EPS = 1e-6
LOG1E4 = math.log(10000.0)
TWO_PI = 2.0 * math.pi
MAGIC = 0x5F3759DF


class Sem:
    def __init__(self, nc, ctx, name):
        self.h = ctx.enter_context(nc.semaphore(name))
        self.v = 0

    def bump(self, k=1):
        self.v += k
        return self.v


class Rec:
    ENGINES = ("sync", "gpsimd", "tensor", "scalar", "vector")

    def __init__(self, nc, chain=None, last=None):
        self.nc = nc
        self.s = {k: [] for k in self.ENGINES}
        self.chain = chain or {}
        self.last = last if last is not None else {}

    def __call__(self, eng, fn):
        self.s[eng].append(fn)

    def op(self, eng, fn, sem=None):
        prev = self.last.get(eng)
        if prev is not None and prev[1] > 0:
            self.wait(eng, prev[0], prev[1])
        if sem is None:
            s = self.chain[eng]
            v = s.bump()
        else:
            s = sem
            v = s.v + 1
        self.s[eng].append(lambda e, f=fn, h=s.h: f(e).then_inc(h, 1))
        self.last[eng] = (s, v)
        return v

    def wait(self, eng, sem, val):
        if val > 0:
            self.s[eng].append(lambda e, s=sem.h, v=val: e.wait_ge(s, v))

    def dma(self, q, sem, out_ap, in_ap):
        v = sem.bump(16)
        self.s[q].append(lambda e, o=out_ap, i=in_ap, s=sem.h: e.dma_start(o, i).then_inc(s, 16))
        return v

    def flush(self):
        with self.nc.Block() as blk:
            for name in self.ENGINES:
                fns = self.s[name]
                if fns:
                    getattr(blk, name)(lambda e, fns=tuple(fns): [f(e) for f in fns])
        self.s = {k: [] for k in self.ENGINES}


def build_program(seq, debug=False):
    assert seq % 1024 == 0
    NCH = seq // 128
    HALF = seq // 2
    HCH = HALF // 128
    HT = HALF // 512
    NQT = seq // 512
    SB = KC // 4

    nc = bass.Bass("TRN2", target_bir_lowering=False)

    x_in = nc.dram_tensor("x", [seq, D], F32, kind="ExternalInput").ap()
    wq = nc.dram_tensor("wqT", [D, D], F32, kind="ExternalInput").ap()
    wk = nc.dram_tensor("wkT", [D, D], F32, kind="ExternalInput").ap()
    wv = nc.dram_tensor("wvT", [D, D], F32, kind="ExternalInput").ap()
    wo = nc.dram_tensor("woT", [D, D], F32, kind="ExternalInput").ap()
    w0 = nc.dram_tensor("w0T", [D, FF], F32, kind="ExternalInput").ap()
    w1 = nc.dram_tensor("w1T", [FF, D], F32, kind="ExternalInput").ap()
    b0_in = nc.dram_tensor("b0", [FF], F32, kind="ExternalInput").ap()
    b1_in = nc.dram_tensor("b1", [D], F32, kind="ExternalInput").ap()
    y_out = nc.dram_tensor("y", [seq, D], F32, kind="ExternalOutput").ap()

    knd = "ExternalOutput" if debug else "Internal"
    if debug:
        dbg_sin = nc.dram_tensor("dbg_sin", [64, seq], F32, kind="ExternalOutput").ap()
        dbg_cos = nc.dram_tensor("dbg_cos", [64, seq], F32, kind="ExternalOutput").ap()
        dbg_wo = nc.dram_tensor("dbg_wo", [seq, D], F32, kind="ExternalOutput").ap()
        dbg_e = nc.dram_tensor("dbg_e", [seq, D], F32, kind="ExternalOutput").ap()
        dbg_dg = nc.dram_tensor("dbg_dg", [128, FNC, 512], BF16, kind="ExternalOutput").ap()
    q_scr = nc.dram_tensor("q_scr", [H, 128, seq], BF16, kind=knd).ap()
    k_scr = nc.dram_tensor("k_scr", [H, 128, seq], BF16, kind=knd).ap()
    v_scr = nc.dram_tensor("v_scr", [H, 128, seq], BF16, kind=knd).ap()
    attn_scr = nc.dram_tensor("attn_scr", [KC, 128, seq], BF16, kind=knd).ap()
    x1_scr = nc.dram_tensor("x1_scr", [seq, D], F32, kind=knd).ap()
    xl2_scr = nc.dram_tensor("xl2_scr", [KC, 128, seq], BF16, kind=knd).ap()
    w0b_scr = nc.dram_tensor("w0b", [D, FF], BF16, kind=knd).ap()
    w1b_scr = nc.dram_tensor("w1b", [FF, D], BF16, kind=knd).ap()

    xv = x_in.rearrange("(c p) d -> c p d", p=128)
    ck = {}

    with ExitStack() as octx:
        def sb(name, shape, dt=F32, ctx=None):
            return (ctx or octx).enter_context(nc.sbuf_tensor(name, shape, dt))

        def pst(name, shape, dt=F32, ctx=None):
            return (ctx or octx).enter_context(nc.psum_tensor(name, shape, dt))

        def S(name, ctx=None):
            return Sem(nc, ctx or octx, name)

        ident = sb("ident", [128, 128], BF16)
        ident64 = sb("ident64", [64, 64], F32)
        ones128 = sb("ones128", [128, 128], BF16)
        ones1f = sb("ones1f", [1, 128], F32)

        b0_sb = sb("b0_sb", [128, FNC], F32)
        b1b = sb("b1b", [128, D], F32)

        s_ld = S("s_ld")
        s_dveS = S("s_dveS")
        s_actS = S("s_actS")
        s_peS = S("s_peS")
        s_xld = [S(f"s_xld{i}") for i in range(2)]
        s_wld = [S(f"s_wld{i}") for i in range(2)]
        s_wdone = S("s_wdone")
        s_mm = S("s_mm")
        s_ev = S("s_ev")
        s_qe = S("s_qe")
        s_qkst = [S(f"s_qkst{i}") for i in range(2)]
        s_sq = S("s_sq")
        s_var = S("s_var")
        s_rstd = S("s_rstd")
        s_xln = S("s_xln")
        s_tp = S("s_tp")
        s_fm = S("s_fm")
        s_cv = [S(f"s_cv{i}") for i in range(2)]
        s_cvo = [S(f"s_cvo{i}") for i in range(2)]
        s_kl = [S(f"s_kl{i}") for i in range(2)]
        s_ql = [S(f"s_ql{i}") for i in range(2)]
        s_vl = [S(f"s_vl{i}") for i in range(2)]
        s_st2 = S("s_st2")
        s_sq2 = S("s_sq2")
        s_s4 = S("s_s4")
        s_av = S("s_av")
        s_dm = S("s_dm")
        s_rc = S("s_rc")
        s_at = S("s_at")
        s_ast = [S(f"s_ast{i}") for i in range(2)]
        s_vt = S("s_vt")
        s_vte = S("s_vte")
        s_hd = S("s_hd")
        s_ain = [S(f"s_ain{i}") for i in range(2)]
        s_wo3 = S("s_wo3")
        s_ev3 = S("s_ev3")
        s_x1 = S("s_x1")
        s_x1st = [S(f"s_x1st{i}") for i in range(2)]
        s_xl2 = S("s_xl2")
        s_x2st = [S(f"s_x2st{i}") for i in range(2)]
        s_xtl = [S(f"s_xtl{i}") for i in range(2)]
        s_x1l = [S(f"s_x1l{i}") for i in range(2)]
        s_w0l = [S(f"s_w0l{i}") for i in range(2)]
        s_w1l = [S(f"s_w1l{i}") for i in range(2)]
        s_f1 = S("s_f1")
        s_ge = S("s_ge")
        s_f2s = S("s_f2s")
        s_f2 = S("s_f2")
        s_ev2 = S("s_ev2")
        s_y = S("s_y")
        s_yst = [S(f"s_yst{i}") for i in range(2)]
        s_wol = S("s_wol")
        s_gps = S("s_gps")
        chain = {"vector": S("s_chv"), "scalar": S("s_chs"), "gpsimd": S("s_chg")}
        last_op = {}

        c12 = octx.enter_context(ExitStack())
        sinf = sb("sinf", [64, seq], F16, c12)
        cosf = sb("cosf", [64, seq], F16, c12)
        cstage = [sb(f"cstage{i}", [128, D], BF16, c12) for i in range(2)]
        masks = sb("masks", [128, 4 * 512], BF16, c12)

        with ExitStack() as c1:
            rec = Rec(nc, chain, last_op)
            b0t = sb("b0t", [64, 128], F32, c1)
            b1row = sb("b1row", [1, D], F32, c1)
            th = sb("th", [64, 1], F32, c1)
            iop = sb("iop", [64, 1], F32, c1)
            posr = sb("posr", [1, seq], F32, c1)
            ubuf = sb("ubuf", [64, seq], F32, c1)
            ibuf = sb("ibuf", [64, seq], I32, c1)
            fbuf = sb("fbuf", [64, seq], F32, c1)
            rsin = sb("rsin", [64, seq], F32, c1)
            rcos = sb("rcos", [64, seq], F32, c1)
            p_s1 = pst("p_s1", [64, 512], F32, c1)
            p_s2 = pst("p_s2", [128, 512], F32, c1)
            p_bt = pst("p_bt", [128, 64], F32, c1)

            g = "gpsimd"
            rec.op(g, lambda e: e.memset(ident[:], 0.0))
            rec.op(g, lambda e: e.affine_select(out=ident[:], in_=ident[:],
                                             compare_op=ALU.not_equal, fill=1.0, base=0,
                                             pattern=[[-1, 128]], channel_multiplier=1))
            rec.op(g, lambda e: e.memset(ident64[:], 0.0))
            rec.op(g, lambda e: e.affine_select(out=ident64[:], in_=ident64[:],
                                             compare_op=ALU.not_equal, fill=1.0, base=0,
                                             pattern=[[-1, 64]], channel_multiplier=1))
            rec.op(g, lambda e: e.memset(ones128[:], 1.0))
            rec.op(g, lambda e: e.memset(ones1f[:], 1.0))
            rec.op(g, lambda e: e.memset(masks[:], 1.0))
            for mi in range(4):
                rec.op(g, lambda e, mi=mi: e.affine_select(
                    out=masks[:, mi * 512:(mi + 1) * 512],
                    in_=masks[:, mi * 512:(mi + 1) * 512],
                    compare_op=ALU.is_ge, fill=0.0, base=-128 * mi,
                    pattern=[[1, 512]], channel_multiplier=-1))
            rec.op(g, lambda e: e.iota(iop[:], [[0, 1]], base=0, channel_multiplier=1,
                                       allow_small_or_imprecise_dtypes=True))
            rec.op(g, lambda e: e.iota(posr[:], [[1, seq]], base=0, channel_multiplier=0,
                                    allow_small_or_imprecise_dtypes=True), s_gps)
            v_gps = s_gps.bump()
            rec.dma(g, s_ld, b0t[:], b0_in.rearrange("(p c) -> p c", p=64))
            vld = rec.dma(g, s_ld, b1row[:], b1_in[None, :])

            rec.wait("vector", s_gps, v_gps)
            rec.wait("tensor", s_gps, v_gps)
            rec.wait("scalar", s_gps, v_gps)
            rec.op("scalar", lambda e: e.activation(th[:], iop[:], AF.Exp,
                                                    scale=-LOG1E4 / 64.0), s_actS)
            v_th = s_actS.bump()

            rec.wait("tensor", s_ld, vld)
            rec("tensor", lambda e: e.transpose(p_bt[:], b0t[:], ident64[:]).then_inc(s_peS.h, 1))
            v_bt = s_peS.bump()
            rec.wait("vector", s_peS, v_bt)
            rec.op("vector", lambda e: e.tensor_copy(b0_sb[:, 0:64], p_bt[:]), s_dveS)
            v_b0c = s_dveS.bump()

            rec.wait("tensor", s_dveS, v_b0c)
            nslab = seq // 512
            for sl in range(nslab + 4):
                if sl < nslab:
                    src = posr[:, sl * 512:(sl + 1) * 512]
                else:
                    src = b1row[:, (sl - nslab) * 512:(sl - nslab + 1) * 512]
                if sl < nslab:
                    rec("tensor", lambda e, s=src: e.matmul(
                        p_s1[:], ones1f[:, 0:64], s, start=True, stop=True).then_inc(s_peS.h, 1))
                else:
                    rec("tensor", lambda e, s=src: e.matmul(
                        p_s2[:], ones1f[:], s, start=True, stop=True).then_inc(s_peS.h, 1))
                vmm = s_peS.bump()
                rec.wait("vector", s_peS, vmm)
                if sl == 0:
                    rec.wait("vector", s_actS, v_th)
                if sl < nslab:
                    sls = slice(sl * 512, (sl + 1) * 512)
                    rec.op("vector", lambda e, s=sls: e.tensor_scalar(
                        ubuf[:, s], p_s1[:], th[:], 1.0 / TWO_PI, ALU.mult, ALU.mult))
                    rec.op("vector", lambda e, s=sls: e.tensor_copy(ibuf[:, s], ubuf[:, s]))
                    rec.op("vector", lambda e, s=sls: e.tensor_copy(fbuf[:, s], ibuf[:, s]))
                    rec.op("vector", lambda e, s=sls: e.tensor_sub(fbuf[:, s], ubuf[:, s], fbuf[:, s]))
                    rec.op("vector", lambda e, s=sls: e.tensor_single_scalar(
                        rsin[:, s], fbuf[:, s], TWO_PI, ALU.mult))
                    rec.op("vector", lambda e, s=sls: e.tensor_single_scalar(
                        ubuf[:, s], ubuf[:, s], 0.25, ALU.add))
                    rec.op("vector", lambda e, s=sls: e.tensor_copy(ibuf[:, s], ubuf[:, s]))
                    rec.op("vector", lambda e, s=sls: e.tensor_copy(fbuf[:, s], ibuf[:, s]))
                    rec.op("vector", lambda e, s=sls: e.tensor_sub(fbuf[:, s], ubuf[:, s], fbuf[:, s]))
                    rec.op("vector", lambda e, s=sls: e.tensor_single_scalar(
                        rcos[:, s], fbuf[:, s], TWO_PI, ALU.mult), s_dveS)
                else:
                    c0 = (sl - nslab) * 512
                    rec.op("vector", lambda e, c0=c0: e.tensor_copy(
                        b1b[:, c0:c0 + 512], p_s2[:]), s_dveS)
                vdv = s_dveS.bump()
                rec.wait("tensor", s_dveS, vdv)
            rec.wait("scalar", s_dveS, s_dveS.v)
            rec.op("scalar", lambda e: e.activation(sinf[:], rsin[:], AF.Sin))
            rec.op("scalar", lambda e: e.activation(cosf[:], rcos[:], AF.Sin), s_actS)
            v_tbl = s_actS.bump()
            if debug:
                rec.wait("gpsimd", s_actS, v_tbl)
                rec.dma("gpsimd", s_ld, dbg_sin, sinf[:])
                rec.dma("gpsimd", s_ld, dbg_cos, cosf[:])
                rec.wait("gpsimd", s_ld, s_ld.v)
            rec.flush()

        if True:
            conv_steps = ([("w0", i, hh) for i in range(KC) for hh in range(4)]
                          + [("w1", i, None) for i in range(FNC)])
            conv_pos = [0]

            def emit_conv(rec, n):
                for _ in range(n):
                    i = conv_pos[0]
                    if i >= len(conv_steps):
                        return
                    conv_pos[0] += 1
                    kind, c, hh = conv_steps[i]
                    sl = i % 2
                    buf = cstage[sl]
                    rec.wait("gpsimd", s_cvo[sl], ck.get(("cvo", sl), 0))
                    if kind == "w0":
                        hs = slice(hh * (FF // 4), (hh + 1) * (FF // 4))
                        vin = rec.dma("gpsimd", s_cv[sl], buf[:], w0[c * 128:(c + 1) * 128, hs])
                        rec.wait("gpsimd", s_cv[sl], vin)
                        vout = rec.dma("gpsimd", s_cvo[sl], w0b_scr[c * 128:(c + 1) * 128, hs], buf[:])
                    else:
                        vin = rec.dma("gpsimd", s_cv[sl], buf[:, 0:D], w1[c * 128:(c + 1) * 128, :])
                        rec.wait("gpsimd", s_cv[sl], vin)
                        vout = rec.dma("gpsimd", s_cvo[sl], w1b_scr[c * 128:(c + 1) * 128, :], buf[:, 0:D])
                    ck[("cvo", sl)] = vout

            with ExitStack() as c2:
                rec = Rec(nc, chain, last_op)
                xbuf = [sb(f"xbuf{i}", [128, D], F32, c2) for i in range(2)]
                sumsq = sb("p1ss", [128, 1], F32, c2)
                xsum = sb("p1s", [128, 1], F32, c2)
                mb = sb("p1m", [128, 1], F32, c2)
                vb = sb("p1v", [128, 1], F32, c2)
                lb = sb("p1l", [128, 1], F32, c2)
                rs = sb("p1r", [128, 1], F32, c2)
                nmr = sb("p1n", [128, 1], F32, c2)
                xlnb = [sb(f"xlnb{i}", [128, D], BF16, c2) for i in range(2)]
                xfm = sb("xfm", [128, KC * HALF], BF16, c2)
                wsb = [sb(f"wsb{i}", [128, KC * 512], BF16, c2) for i in range(2)]
                rt1 = sb("rt1", [64, 512], F16, c2)
                rt2 = sb("rt2", [64, 512], F16, c2)
                qlo16 = [sb(f"qlo16{i}", [64, 512], F16, c2) for i in range(2)]
                qhi16 = [sb(f"qhi16{i}", [64, 512], F16, c2) for i in range(2)]
                roblo = [sb(f"roblo{i}", [64, HALF], BF16, c2) for i in range(2)]
                robhi = [sb(f"robhi{i}", [64, HALF], BF16, c2) for i in range(2)]
                p_pr = [pst(f"p_pr{i}", [128, 512], F32, c2) for i in range(4)]
                p_tp = [pst(f"p_tp{i}", [128, 1024], BF16, c2) for i in range(4)]

                wT = {"q": wq, "k": wk, "v": wv}
                scr = {"q": q_scr, "k": k_scr, "v": v_scr}
                xfm3 = xfm[:].rearrange("p (k t) -> p k t", k=KC)
                gi_ctr = [0]
                tp_ctr = [0]

                for hf in range(2):
                    for c in range(HCH):
                        gc = hf * HCH + c
                        bi = gc % 2
                        rec.wait("sync", s_xln, ck.get(("xbuf", bi), 0))
                        vx = rec.dma("sync", s_xld[bi], xbuf[bi][:], xv[gc])
                        rec.wait("scalar", s_xld[bi], vx)
                        rec.wait("scalar", s_var, ck.get("p1var", 0))
                        rec.wait("scalar", s_tp, ck.get(("xlnb", bi), 0))
                        rec.op("scalar", lambda e, b=bi: e.activation(
                            xlnb[b][:], xbuf[b][:], AF.Square, accum_out=sumsq[:]), s_sq)
                        vsq = s_sq.bump()
                        rec.wait("vector", s_xld[bi], vx)
                        rec.wait("vector", s_rstd, ck.get("p1rstd", 0))
                        rec.op("vector", lambda e, b=bi: e.reduce_sum(xsum[:], xbuf[b][:], axis=AX.X))
                        rec.wait("vector", s_sq, vsq)
                        rec.op("vector", lambda e: e.tensor_single_scalar(mb[:], xsum[:], 1.0 / D, ALU.mult))
                        rec.op("vector", lambda e: e.tensor_single_scalar(vb[:], sumsq[:], 1.0 / D, ALU.mult))
                        rec.op("vector", lambda e: e.tensor_mul(xsum[:], mb[:], mb[:]))
                        rec.op("vector", lambda e: e.tensor_sub(vb[:], vb[:], xsum[:]))
                        rec.op("vector", lambda e: e.tensor_single_scalar(
                            vb[:], vb[:], LN_EPS, ALU.add), s_var)
                        ck["p1var"] = s_var.bump()
                        rec.wait("scalar", s_var, ck["p1var"])
                        rec.op("scalar", lambda e: e.activation(lb[:], vb[:], AF.Ln))
                        rec.op("scalar", lambda e: e.activation(
                            rs[:], lb[:], AF.Exp, scale=-0.5), s_rstd)
                        ck["p1rstd"] = s_rstd.bump()
                        rec.wait("vector", s_rstd, ck["p1rstd"])
                        rec.wait("vector", s_tp, ck.get(("xlnb", bi), 0))
                        rec.op("vector", lambda e: e.scalar_tensor_tensor(
                            nmr[:], mb[:], -1.0, rs[:], ALU.mult, ALU.mult))
                        rec.op("vector", lambda e, b=bi: e.tensor_scalar(
                            xlnb[b][:], xbuf[b][:], rs[:], nmr[:], ALU.mult, ALU.add), s_xln)
                        ck[("xbuf", bi)] = s_xln.bump()
                        rec.wait("tensor", s_xln, ck[("xbuf", bi)])
                        for grp in range(2):
                            ti = tp_ctr[0]
                            tp_ctr[0] += 1
                            pt = p_tp[ti % 4]
                            rec.wait("tensor", s_fm, ck.get(("ptp", ti % 4), 0))
                            for j in range(8):
                                kcc = grp * 8 + j
                                if j < 7:
                                    rec("tensor", lambda e, b=bi, k=kcc, p=pt, j=j: e.transpose(
                                        p[:, j * 128:(j + 1) * 128],
                                        xlnb[b][:, k * 128:(k + 1) * 128], ident[:]))
                                else:
                                    rec("tensor", lambda e, b=bi, k=kcc, p=pt, j=j: e.transpose(
                                        p[:, j * 128:(j + 1) * 128],
                                        xlnb[b][:, k * 128:(k + 1) * 128],
                                        ident[:]).then_inc(s_tp.h, 1))
                            vtp = s_tp.bump()
                            if grp == 1:
                                ck[("xlnb", bi)] = vtp
                            rec.wait("scalar", s_tp, vtp)
                            dst = xfm3[:, grp * 8:(grp + 1) * 8, c * 128:(c + 1) * 128]
                            srcp = pt[:].rearrange("p (j t) -> p j t", j=8)
                            rec.op("scalar", lambda e, d=dst, s=srcp: e.activation(
                                d, s, AF.Identity), s_fm)
                            ck[("ptp", ti % 4)] = s_fm.bump()
                    ck[("fmhalf", hf)] = s_fm.v

                    for sbi in range(3 * SB):
                        proj = "qkv"[sbi // SB]
                        blkc = sbi % SB
                        gsb = hf * 3 * SB + sbi
                        wbi = gsb % 2
                        rec.wait("gpsimd", s_qe, ck.get(("wsb", wbi), 0))
                        wsrc = wT[proj].rearrange("(k p) n -> p k n", p=128)[:, :, blkc * 512:(blkc + 1) * 512]
                        vw = rec.dma("gpsimd", s_wld[wbi],
                                     wsb[wbi][:].rearrange("p (k n) -> p k n", k=KC), wsrc)
                        rec.wait("tensor", s_wld[wbi], vw)
                        if sbi == 0:
                            rec.wait("tensor", s_fm, ck[("fmhalf", hf)])
                        wv3 = wsb[wbi][:].rearrange("p (k n) -> p k n", k=KC)
                        for dqc in range(4):
                            dqg = blkc * 4 + dqc
                            slot = dqg % 2
                            for t in range(HT):
                                gi = gi_ctr[0]
                                gi_ctr[0] += 1
                                pb = p_pr[gi % 4]
                                rec.wait("tensor", s_qe, ck.get(("ppr", gi % 4), 0))
                                for kcc in range(KC):
                                    l_ap = wv3[:, kcc, dqc * 128:(dqc + 1) * 128]
                                    r_ap = xfm3[:, kcc, t * 512:(t + 1) * 512]
                                    stf = kcc == 0
                                    if kcc == KC - 1:
                                        rec("tensor", lambda e, o=pb, l=l_ap, r=r_ap, s=stf: e.matmul(
                                            o[:], l, r, start=s, stop=True).then_inc(s_mm.h, 1))
                                    else:
                                        rec("tensor", lambda e, o=pb, l=l_ap, r=r_ap, s=stf: e.matmul(
                                            o[:], l, r, start=s, stop=False))
                                vmm = s_mm.bump()
                                cols = slice(t * 512, (t + 1) * 512)
                                gcols = slice(hf * HALF + t * 512, hf * HALF + (t + 1) * 512)
                                rlo, rhi = roblo[slot], robhi[slot]
                                g2 = gi % 2
                                rec.wait("scalar", s_mm, vmm)
                                if proj in ("q", "k"):
                                    rec.wait("scalar", s_ev, ck.get(("q16", g2), 0))
                                    rec.op("scalar", lambda e, p=pb, o=qlo16[g2]: e.activation(
                                        o[:], p[0:64, :], AF.Identity))
                                    rec.op("scalar", lambda e, p=pb, o=qhi16[g2]: e.activation(
                                        o[:], p[64:128, :], AF.Identity), s_qe)
                                    vqe = s_qe.bump()
                                    ck[("ppr", gi % 4)] = vqe
                                    rec.wait("vector", s_qe, vqe)
                                    if t == 0:
                                        rec.wait("vector", s_qkst[slot], ck.get(("rob", slot), 0))
                                    rec.op("vector", lambda e, q=qlo16[g2], g=gcols: e.tensor_mul(
                                        rt1[:], q[:], cosf[:, g]))
                                    rec.op("vector", lambda e, q=qhi16[g2], g=gcols: e.tensor_mul(
                                        rt2[:], q[:], sinf[:, g]))
                                    rec.op("vector", lambda e, r=rlo, c=cols: e.tensor_sub(
                                        r[:, c], rt1[:], rt2[:]))
                                    rec.op("vector", lambda e, q=qlo16[g2], g=gcols: e.tensor_mul(
                                        rt1[:], q[:], sinf[:, g]))
                                    rec.op("vector", lambda e, q=qhi16[g2], g=gcols: e.tensor_mul(
                                        rt2[:], q[:], cosf[:, g]))
                                    rec.op("vector", lambda e, r=rhi, c=cols: e.tensor_add(
                                        r[:, c], rt1[:], rt2[:]), s_ev)
                                    ck[("q16", g2)] = s_ev.bump()
                                    if t == HT - 1:
                                        rec.wait("scalar", s_ev, ck[("q16", g2)])
                                        hc = slice(hf * HALF, (hf + 1) * HALF)
                                        rec.dma("scalar", s_qkst[slot],
                                                scr[proj][dqg, 0:64, hc], rlo[:])
                                        ck[("rob", slot)] = rec.dma(
                                            "scalar", s_qkst[slot],
                                            scr[proj][dqg, 64:128, hc], rhi[:])
                                else:
                                    if t == 0:
                                        rec.wait("scalar", s_qkst[slot], ck.get(("rob", slot), 0))
                                    rec.op("scalar", lambda e, p=pb, r=rlo, c=cols: e.activation(
                                        r[:, c], p[0:64, :], AF.Identity))
                                    rec.op("scalar", lambda e, p=pb, r=rhi, c=cols: e.activation(
                                        r[:, c], p[64:128, :], AF.Identity), s_qe)
                                    ck[("ppr", gi % 4)] = s_qe.bump()
                                    if t == HT - 1:
                                        hc = slice(hf * HALF, (hf + 1) * HALF)
                                        rec.dma("scalar", s_qkst[slot],
                                                scr[proj][dqg, 0:64, hc], rlo[:])
                                        ck[("rob", slot)] = rec.dma(
                                            "scalar", s_qkst[slot],
                                            scr[proj][dqg, 64:128, hc], rhi[:])
                        ck[("wsb", wbi)] = s_qe.v
                    emit_conv(rec, 25)

                rec.wait("scalar", s_qkst[0], s_qkst[0].v)
                rec.wait("scalar", s_qkst[1], s_qkst[1].v)
                rec.wait("sync", s_xld[0], s_xld[0].v)
                rec.wait("sync", s_xld[1], s_xld[1].v)
                rec.wait("gpsimd", s_wld[0], s_wld[0].v)
                rec.wait("gpsimd", s_wld[1], s_wld[1].v)
                rec.flush()

            with ExitStack() as c3:
                rec = Rec(nc, chain, last_op)
                kb = [sb(f"kb{i}", [128, seq], BF16, c3) for i in range(2)]
                qb = [sb(f"qb{i}", [128, seq], BF16, c3) for i in range(2)]
                vbf = [sb(f"vb{i}", [128, seq], BF16, c3) for i in range(2)]
                vtm = [sb(f"vtm{i}", [128, seq], BF16, c3) for i in range(2)]
                s2b = [sb(f"s2b{i}", [128, 512], BF16, c3) for i in range(5)]
                s2m = sb("s2m", [128, 512], BF16, c3)
                s4b = [sb(f"s4b{i}", [128, 512], BF16, c3) for i in range(5)]
                dmb = [sb(f"dmb{i}", [128, 512], F32, c3) for i in range(2)]
                rcb = [sb(f"rcb{i}", [128, 512], F32, c3) for i in range(2)]
                atb = [sb(f"atb{i}", [128, 512], BF16, c3) for i in range(2)]
                p_st = [pst(f"p_st{i}", [128, 512], F32, c3) for i in range(4)]
                p_o = [pst(f"p_o{i}", [128, 512], F32, c3) for i in range(2)]
                p_d = [pst(f"p_d{i}", [128, 512], F32, c3) for i in range(1)]
                p_vt = pst("p_vt", [128, 1024], BF16, c3)

                emit_conv(rec, len(conv_steps))
                n2 = [0]
                nqt = [0]

                def emit_chunk(h, bi, qt, jc):
                    i = n2[0] + jc
                    pb = p_st[i % 4]
                    rec("tensor", lambda e, p=pb, b=bi, jc=jc, q0=qt * 512: e.matmul(
                        p[:], kb[b][:, jc * 128:(jc + 1) * 128], qb[b][:, q0:q0 + 512],
                        start=True, stop=True).then_inc(s_st2.h, 1))
                    vst = s_st2.bump()
                    rec.wait("scalar", s_st2, vst)
                    rec.wait("scalar", s_s4, ck.get(("s2b", i % 5), 0))
                    rec("scalar", lambda e, p=pb, o=s2b[i % 5]: e.activation(
                        o[:], p[:], AF.Square).then_inc(s_sq2.h, 1))
                    rec.last["scalar"] = (s_sq2, s_sq2.v + 1)
                    vs2 = s_sq2.bump()
                    rec.wait("vector", s_sq2, vs2)
                    rec.wait("vector", s_av, ck.get(("s4b", i % 5), 0))
                    dmi = jc - 4 * qt
                    if dmi >= 0:
                        rec.op("vector", lambda e, s=s2b[i % 5], m=dmi: e.tensor_mul(
                            s2m[:], s[:], masks[:, m * 512:(m + 1) * 512]))
                        rec.op("vector", lambda e, s=s2b[i % 5], o=s4b[i % 5]: e.tensor_mul(
                            o[:], s[:], s2m[:]), s_s4)
                        ck[("s2b", i % 5)] = s_s4.bump()
                    else:
                        rec("vector", lambda e, s=s2b[i % 5], o=s4b[i % 5]: e.tensor_mul(
                            o[:], s[:], s[:]).then_inc(s_s4.h, 1))
                        rec.last["vector"] = (s_s4, s_s4.v + 1)
                        ck[("s2b", i % 5)] = s_s4.bump()
                    return ck[("s2b", i % 5)], i

                NG_VT = seq // 1024

                def emit_vt_group(h, grp, first):
                    b = h % 2
                    if first:
                        rec.wait("tensor", s_vl[b], ck[("vload", h)])
                    rec.wait("tensor", s_vte, ck.get("pvt", 0))
                    for j in range(8):
                        cc = grp * 8 + j
                        if j < 7:
                            rec("tensor", lambda e, b=b, c=cc, j=j: e.transpose(
                                p_vt[:, j * 128:(j + 1) * 128],
                                vbf[b][:, c * 128:(c + 1) * 128], ident[:]))
                        else:
                            rec("tensor", lambda e, b=b, c=cc, j=j: e.transpose(
                                p_vt[:, j * 128:(j + 1) * 128],
                                vbf[b][:, c * 128:(c + 1) * 128], ident[:]).then_inc(s_vt.h, 1))
                    vvt = s_vt.bump()
                    rec.wait("scalar", s_vt, vvt)
                    dst = vtm[b][:].rearrange("p (g j t) -> p g j t", g=NG_VT, j=8)[:, grp]
                    srcp = p_vt[:].rearrange("p (j t) -> p j t", j=8)
                    rec.op("scalar", lambda e, d=dst, s=srcp: e.activation(
                        d, s, AF.Identity), s_vte)
                    ck["pvt"] = s_vte.bump()
                    ck[("vte_head", h)] = ck["pvt"]

                def rec_head_loads(hh):
                    b = hh % 2
                    rec.wait("sync", s_at, ck.get(("kqv", b), 0))
                    ck[("kload", hh)] = rec.dma("sync", s_kl[b], kb[b][:], k_scr[hh])
                    ck[("qload", hh)] = rec.dma("sync", s_ql[b], qb[b][:], q_scr[hh])
                    ck[("vload", hh)] = rec.dma("sync", s_vl[b], vbf[b][:], v_scr[hh])

                rec_head_loads(0)
                for h in range(H):
                    bi = h % 2
                    if h + 1 < H:
                        rec_head_loads(h + 1)
                    if ("vte_head", h) not in ck:
                        for grp in range(NG_VT):
                            emit_vt_group(h, grp, grp == 0)
                    vte_head = ck[("vte_head", h)]

                    rec.wait("tensor", s_kl[bi], ck[("kload", h)])
                    rec.wait("tensor", s_ql[bi], ck[("qload", h)])
                    for qt in range(NQT):
                        njc = 4 * qt + 4
                        poi = nqt[0] % 2
                        LAG = 4
                        vs4 = {}
                        for j in range(min(LAG, njc)):
                            vs4[j] = emit_chunk(h, bi, qt, j)
                        for jc in range(njc):
                            v4, i4 = vs4[jc]
                            rec.wait("tensor", s_s4, v4)
                            if jc == 0:
                                rec.wait("tensor", s_vte, vte_head)
                                rec.wait("tensor", s_at, ck.get(("po", poi), 0))
                                rec.wait("tensor", s_dm, ck.get("pd2", 0))
                            last = jc == njc - 1
                            rec("tensor", lambda e, po=p_o[poi], jc=jc, s4=s4b[i4 % 5],
                                vt=vtm[bi], st=(jc == 0), sp=last: e.matmul(
                                po[:], vt[:, jc * 128:(jc + 1) * 128], s4[:], start=st, stop=sp))
                            rec("tensor", lambda e, pd=p_d[0], s4=s4b[i4 % 5],
                                st=(jc == 0), sp=last: e.matmul(
                                pd[:], ones128[:], s4[:], start=st, stop=sp).then_inc(s_av.h, 1))
                            ck[("s4b", i4 % 5)] = s_av.bump()
                            if jc + LAG < njc:
                                vs4[jc + LAG] = emit_chunk(h, bi, qt, jc + LAG)
                        if h + 1 < H and 2 <= qt < 2 + NG_VT and NQT > 2 + NG_VT:
                            emit_vt_group(h + 1, qt - 2, qt == 2)
                        n2[0] += njc
                        vav = s_av.v
                        rec.wait("vector", s_av, vav)
                        rec.wait("vector", s_rc, ck.get(("dmb", poi), 0))
                        rec.op("vector", lambda e, p=p_d[0], o=dmb[poi]: e.tensor_single_scalar(
                            o[:], p[:], ATTN_EPS, ALU.max), s_dm)
                        vdm = s_dm.bump()
                        ck["pd2"] = vdm
                        rec.wait("scalar", s_dm, vdm)
                        rec.wait("scalar", s_at, ck.get(("rcb", poi), 0))
                        rec.op("scalar", lambda e, i=poi: e.activation(rcb[i][:], dmb[i][:], AF.Ln))
                        rec.op("scalar", lambda e, i=poi: e.activation(
                            rcb[i][:], rcb[i][:], AF.Exp, scale=-1.0), s_rc)
                        ck[("dmb", poi)] = s_rc.bump()
                        rec.wait("vector", s_rc, ck[("dmb", poi)])
                        rec.wait("vector", s_ast[poi], ck.get(("atb", poi), 0))
                        rec.op("vector", lambda e, i=poi: e.tensor_mul(
                            atb[i][:], p_o[i][:], rcb[i][:]), s_at)
                        vat = s_at.bump()
                        ck[("rcb", poi)] = vat
                        ck[("po", poi)] = vat
                        rec.wait("scalar", s_at, vat)
                        ck[("atb", poi)] = rec.dma("scalar", s_ast[poi],
                                                   attn_scr[h, :, qt * 512:(qt + 1) * 512], atb[poi][:])
                        nqt[0] += 1
                    ck[("kqv", bi)] = s_at.v

                rec.wait("scalar", s_ast[0], s_ast[0].v)
                rec.wait("scalar", s_ast[1], s_ast[1].v)
                rec.wait("gpsimd", s_cvo[0], s_cvo[0].v)
                rec.wait("gpsimd", s_cvo[1], s_cvo[1].v)
                rec.wait("sync", s_kl[0], s_kl[0].v)
                rec.wait("sync", s_kl[1], s_kl[1].v)
                rec.wait("sync", s_ql[0], s_ql[0].v)
                rec.wait("sync", s_ql[1], s_ql[1].v)
                rec.wait("sync", s_vl[0], s_vl[0].v)
                rec.wait("sync", s_vl[1], s_vl[1].v)
                rec.flush()

        c12.close()

        with ExitStack() as c4:
            rec = Rec(nc, chain, last_op)
            wot = sb("wot", [128, KC * D], BF16, c4)
            ain = [sb(f"ain{i}", [128, KC * 512], BF16, c4) for i in range(2)]
            xb3 = [sb(f"xb3{i}", [128, D], F32, c4) for i in range(2)]
            wsb3 = [sb(f"wsb3{i}", [128, D], F32, c4) for i in range(2)]
            qsum2 = sb("qsum2", [128, 8], F32, c4)
            sm3 = sb("sm3", [128, 12], F32, c4)
            lnb = sb("lnb", [128, D], F32, c4)
            sq3 = sb("sq3d", [128, D], F32, c4)
            x1b = [sb(f"x1b{i}", [128, D], F32, c4) for i in range(2)]
            xl2t = [sb(f"xl2t{i}", [128, D], BF16, c4) for i in range(2)]
            xl2f2 = [sb(f"xl2f2{i}", [128, KC * 128], BF16, c4) for i in range(2)]
            p_q3 = [pst(f"p_q3{i}", [128, 512], F32, c4) for i in range(6)]
            p_t3 = [pst(f"p_t3{i}", [128, 1024], BF16, c4) for i in range(2)]

            vwo = rec.dma("gpsimd", s_wol, wot[:].rearrange("p (k n) -> p k n", k=KC),
                          wo.rearrange("(k p) n -> p k n", p=128))
            wot3 = wot[:].rearrange("p (k n) -> p k n", k=KC)
            nq3 = [0]
            tp3 = [0]

            def emit_tp3(tce):
                b = tce % 2
                rec.wait("tensor", s_xl2, ck[("xl2done", tce)])
                for grp in range(2):
                    ti = tp3[0]
                    tp3[0] += 1
                    pt = p_t3[ti % 2]
                    rec.wait("tensor", s_fm, ck.get(("pt3", ti % 2), 0))
                    for j in range(8):
                        kcc = grp * 8 + j
                        if j < 7:
                            rec("tensor", lambda e, b=b, k=kcc, p=pt, j=j: e.transpose(
                                p[:, j * 128:(j + 1) * 128],
                                xl2t[b][:, k * 128:(k + 1) * 128], ident[:]))
                        else:
                            rec("tensor", lambda e, b=b, k=kcc, p=pt, j=j: e.transpose(
                                p[:, j * 128:(j + 1) * 128],
                                xl2t[b][:, k * 128:(k + 1) * 128], ident[:]).then_inc(s_tp.h, 1))
                    vtp = s_tp.bump()
                    if grp == 1:
                        ck[("xl2t", b)] = vtp
                    rec.wait("scalar", s_tp, vtp)
                    if grp == 0:
                        rec.wait("scalar", s_x2st[b], ck.get(("xl2f2", b), 0))
                    dst = xl2f2[b][:].rearrange("p (k t) -> p k t", k=KC)[:, grp * 8:(grp + 1) * 8, :]
                    srcp = pt[:].rearrange("p (j t) -> p j t", j=8)
                    rec.op("scalar", lambda e, d=dst, s=srcp: e.activation(
                        d, s, AF.Identity), s_fm)
                    ck[("pt3", ti % 2)] = s_fm.bump()
                rec.wait("scalar", s_fm, ck[("pt3", (tp3[0] - 1) % 2)])
                ck[("xl2f2", b)] = rec.dma(
                    "scalar", s_x2st[b],
                    xl2_scr.rearrange("k p n -> p k n")[:, :, tce * 128:(tce + 1) * 128],
                    xl2f2[b][:].rearrange("p (k t) -> p k t", k=KC))
            for gidx in range(seq // 512):
                gbi = gidx % 2
                rec.wait("sync", s_ev3, ck.get(("ain", gbi), 0))
                va = rec.dma("sync", s_ain[gbi], ain[gbi][:].rearrange("p (k t) -> p k t", k=KC),
                             attn_scr.rearrange("k p n -> p k n")[:, :, gidx * 512:(gidx + 1) * 512])
                ain3 = ain[gbi][:].rearrange("p (k t) -> p k t", k=KC)
                for tcl in range(4):
                    tc = gidx * 4 + tcl
                    bi = tc % 2
                    Q0 = qsum2[:, 4 * bi + 0:4 * bi + 1]
                    Q1 = qsum2[:, 4 * bi + 1:4 * bi + 2]
                    Q2 = qsum2[:, 4 * bi + 2:4 * bi + 3]
                    Q3 = qsum2[:, 4 * bi + 3:4 * bi + 4]
                    SS3 = sm3[:, 6 * bi + 0:6 * bi + 1]
                    M3 = sm3[:, 6 * bi + 1:6 * bi + 2]
                    V3 = sm3[:, 6 * bi + 2:6 * bi + 3]
                    L3 = sm3[:, 6 * bi + 3:6 * bi + 4]
                    RS3 = sm3[:, 6 * bi + 4:6 * bi + 5]
                    NM3 = sm3[:, 6 * bi + 5:6 * bi + 6]
                    rec.wait("gpsimd", s_x1, ck.get(("xb3", bi), 0))
                    vx = rec.dma("gpsimd", s_xld[bi], xb3[bi][:], xv[tc])
                    rec.wait("tensor", s_ain[gbi], va)
                    if tc == 0:
                        rec.wait("tensor", s_wol, vwo)
                    for qd in range(4):
                        qi = nq3[0]
                        nq3[0] += 1
                        pqb = p_q3[qi % 6]
                        rec.wait("tensor", s_ev3, ck.get(("pq3", qi % 6), 0))
                        for kcc in range(KC):
                            l_ap = ain3[:, kcc, tcl * 128:(tcl + 1) * 128]
                            r_ap = wot3[:, kcc, qd * 512:(qd + 1) * 512]
                            stf = kcc == 0
                            if kcc == KC - 1:
                                rec("tensor", lambda e, o=pqb, l=l_ap, r=r_ap, s=stf: e.matmul(
                                    o[:], l, r, start=s, stop=True).then_inc(s_wo3.h, 1))
                            else:
                                rec("tensor", lambda e, o=pqb, l=l_ap, r=r_ap, s=stf: e.matmul(
                                    o[:], l, r, start=s, stop=False))
                        vq3 = s_wo3.bump()
                        rec.wait("scalar", s_wo3, vq3)
                        if qd == 0:
                            rec.wait("scalar", s_xl2, ck.get(("p3adve", bi), 0))
                        rec.op("scalar", lambda e, p=pqb, b=bi, qd=qd: e.activation(
                            wsb3[b][:, qd * 512:(qd + 1) * 512], p[:], AF.Identity,
                            accum_out=qsum2[:, 4 * bi + qd:4 * bi + qd + 1]), s_ev3)
                        ck[("pq3", qi % 6)] = s_ev3.bump()
                    vev = s_ev3.v
                    rec.wait("scalar", s_ev3, vev)
                    if debug:
                        rec.dma("scalar", s_x1st[bi], dbg_wo[tc * 128:(tc + 1) * 128, :], wsb3[bi][:])
                    rec.op("scalar", lambda e, b=bi: e.activation(
                        sq3[:], wsb3[b][:], AF.Square, accum_out=SS3), s_sq)
                    vsq = s_sq.bump()
                    rec.wait("vector", s_sq, vsq)
                    rec.op("vector", lambda e: e.tensor_add(Q0, Q0, Q1))
                    rec.op("vector", lambda e: e.tensor_add(Q2, Q2, Q3))
                    rec.op("vector", lambda e: e.tensor_add(Q0, Q0, Q2))
                    rec.op("vector", lambda e: e.tensor_single_scalar(M3, Q0, 1.0 / D, ALU.mult))
                    rec.op("vector", lambda e: e.tensor_single_scalar(V3, SS3, 1.0 / D, ALU.mult))
                    rec.op("vector", lambda e: e.tensor_mul(Q1, M3, M3))
                    rec.op("vector", lambda e: e.tensor_sub(V3, V3, Q1))
                    rec.op("vector", lambda e: e.tensor_single_scalar(
                        V3, V3, LN_EPS, ALU.add), s_var)
                    vv3 = s_var.bump()
                    rec.wait("scalar", s_var, vv3)
                    rec.op("scalar", lambda e: e.activation(L3, V3, AF.Ln))
                    rec.op("scalar", lambda e: e.activation(
                        RS3, L3, AF.Exp, scale=-0.5), s_rstd)
                    vr3 = s_rstd.bump()
                    rec.wait("vector", s_rstd, vr3)
                    rec.wait("vector", s_xld[bi], vx)
                    rec.wait("vector", s_x1st[bi], ck.get(("x1b", bi), 0))
                    rec.op("vector", lambda e: e.scalar_tensor_tensor(
                        NM3, M3, -1.0, RS3, ALU.mult, ALU.mult))
                    rec.op("vector", lambda e, b=bi: e.tensor_scalar(
                        lnb[:], wsb3[b][:], RS3, NM3, ALU.mult, ALU.add))
                    rec.op("vector", lambda e, b=bi: e.tensor_add(
                        x1b[b][:], lnb[:], xb3[b][:]), s_x1)
                    ck[("xb3", bi)] = s_x1.bump()
                    rec.wait("scalar", s_x1, ck[("xb3", bi)])
                    ck[("x1b", bi)] = rec.dma("scalar", s_x1st[bi],
                                              x1_scr[tc * 128:(tc + 1) * 128, :], x1b[bi][:])
                    rec.op("scalar", lambda e, b=bi: e.activation(
                        sq3[:], x1b[b][:], AF.Square, accum_out=SS3), s_sq)
                    vsq2 = s_sq.bump()
                    rec.wait("vector", s_sq, vsq2)
                    rec.op("vector", lambda e, b=bi: e.reduce_sum(Q0, x1b[b][:], axis=AX.X))
                    rec.op("vector", lambda e: e.tensor_single_scalar(M3, Q0, 1.0 / D, ALU.mult))
                    rec.op("vector", lambda e: e.tensor_single_scalar(V3, SS3, 1.0 / D, ALU.mult))
                    rec.op("vector", lambda e: e.tensor_mul(Q1, M3, M3))
                    rec.op("vector", lambda e: e.tensor_sub(V3, V3, Q1))
                    rec.op("vector", lambda e: e.tensor_single_scalar(
                        V3, V3, LN_EPS, ALU.add), s_var)
                    vv4 = s_var.bump()
                    rec.wait("scalar", s_var, vv4)
                    rec.op("scalar", lambda e: e.activation(L3, V3, AF.Ln))
                    rec.op("scalar", lambda e: e.activation(
                        RS3, L3, AF.Exp, scale=-0.5), s_rstd)
                    vr4 = s_rstd.bump()
                    rec.wait("vector", s_rstd, vr4)
                    rec.wait("vector", s_tp, ck.get(("xl2t", bi), 0))
                    rec.op("vector", lambda e: e.scalar_tensor_tensor(
                        NM3, M3, -1.0, RS3, ALU.mult, ALU.mult))
                    rec.op("vector", lambda e, b=bi: e.tensor_scalar(
                        xl2t[b][:], x1b[b][:], RS3, NM3, ALU.mult, ALU.add), s_xl2)
                    ck[("p3adve", bi)] = s_xl2.bump()
                    ck[("xl2done", tc)] = ck[("p3adve", bi)]
                    if tc >= 1:
                        emit_tp3(tc - 1)
                    if tc == NCH - 1:
                        emit_tp3(tc)
                ck[("ain", gbi)] = s_ev3.v

            rec.wait("scalar", s_x1st[0], s_x1st[0].v)
            rec.wait("scalar", s_x1st[1], s_x1st[1].v)
            rec.wait("scalar", s_x2st[0], s_x2st[0].v)
            rec.wait("scalar", s_x2st[1], s_x2st[1].v)
            rec.wait("sync", s_ain[0], s_ain[0].v)
            rec.wait("sync", s_ain[1], s_ain[1].v)
            rec.wait("gpsimd", s_xld[0], s_xld[0].v)
            rec.wait("gpsimd", s_xld[1], s_xld[1].v)
            rec.wait("gpsimd", s_wol, s_wol.v)
            rec.flush()

        with ExitStack() as c5:
            rec = Rec(nc, chain, last_op)
            dgb = sb("dgb", [128, FNC * 512], BF16, c5)
            xt3b = [sb(f"xt3{i}", [128, KC * 512], BF16, c5) for i in range(2)]
            w0bu = [sb(f"w0bu{i}", [128, KC * 256], BF16, c5) for i in range(2)]
            w1bu = [sb(f"w1bu{i}", [128, 8 * 512], BF16, c5) for i in range(2)]
            x1c = [sb(f"x1c{i}", [128, D], F32, c5) for i in range(2)]
            esb = [sb(f"esb{i}", [128, D], F32, c5) for i in range(4)]
            yb = [sb(f"yb{i}", [128, D], F32, c5) for i in range(2)]
            sq5 = yb
            epart = sb("epart", [128, 16], F32, c5)
            sm_m = sb("sm_m", [128, 4], F32, c5)
            sm_v = sb("sm_v", [128, 4], F32, c5)
            sm_y = sb("sm_y", [128, 4], F32, c5)
            sm_z = sb("sm_z", [128, 4], F32, c5)
            sm_i = sb("sm_i", [128, 4], I32, c5)
            sm_n = sb("sm_n", [128, 4], F32, c5)
            p_f1 = [pst(f"p_f1{i}", [128, 512], F32, c5) for i in range(3)]
            p_f2 = [pst(f"p_f2{i}", [128, 512], F32, c5) for i in range(4)]

            w0v = w0b_scr.rearrange("(k p) n -> p k n", p=128)
            w1v = w1b_scr.rearrange("(c p) n -> p c n", p=128)
            x1v = x1_scr.rearrange("(c p) d -> c p d", p=128)
            yv = y_out.rearrange("(c p) d -> c p d", p=128)
            nf1 = [0]
            nw0 = [0]
            nw1 = [0]

            for t in range(seq // 512):
                xti = t % 2
                rec.wait("sync", s_ge, ck.get(("xt3", xti), 0))
                vxt = rec.dma("sync", s_xtl[xti], xt3b[xti][:].rearrange("p (k t) -> p k t", k=KC),
                              xl2_scr.rearrange("k p n -> p k n")[:, :, t * 512:(t + 1) * 512])
                vx1_t = {}
                for tc in range(2):
                    tcg = t * 4 + tc
                    xbi = tcg % 2
                    rec.wait("gpsimd", s_y, ck.get(("x1c", xbi), 0))
                    vx1_t[tc] = (xbi, rec.dma("gpsimd", s_x1l[xbi], x1c[xbi][:], x1v[tcg]))
                xt33 = xt3b[xti][:].rearrange("p (k t) -> p k t", k=KC)
                for blk in range(FF // 256):
                    wbi = nw0[0] % 2
                    nw0[0] += 1
                    rec.wait("sync", s_ge, ck.get(("w0bu", wbi), 0))
                    vw0 = rec.dma("sync", s_w0l[wbi],
                                  w0bu[wbi][:].rearrange("p (k n) -> p k n", k=KC),
                                  w0v[:, :, blk * 256:(blk + 1) * 256])
                    rec.wait("tensor", s_w0l[wbi], vw0)
                    if blk == 0:
                        rec.wait("tensor", s_xtl[xti], vxt)
                    w03 = w0bu[wbi][:].rearrange("p (k n) -> p k n", k=KC)
                    for dnc in range(2):
                        ncg = blk * 2 + dnc
                        fi = nf1[0]
                        nf1[0] += 1
                        pb = p_f1[fi % 3]
                        rec.wait("tensor", s_ge, ck.get(("pf1", fi % 3), 0))
                        for kcc in range(KC):
                            l_ap = w03[:, kcc, dnc * 128:(dnc + 1) * 128]
                            r_ap = xt33[:, kcc, :]
                            stf = kcc == 0
                            if kcc == KC - 1:
                                rec("tensor", lambda e, o=pb, l=l_ap, r=r_ap, s=stf: e.matmul(
                                    o[:], l, r, start=s, stop=True).then_inc(s_f1.h, 1))
                            else:
                                rec("tensor", lambda e, o=pb, l=l_ap, r=r_ap, s=stf: e.matmul(
                                    o[:], l, r, start=s, stop=False))
                        vf1 = s_f1.bump()
                        rec.wait("scalar", s_f1, vf1)
                        if ncg == 0 and t >= 1:
                            rec.wait("scalar", s_ev2, ck["dgdone"])
                        rec.op("scalar", lambda e, p=pb, n=ncg: e.activation(
                            dgb[:, n * 512:(n + 1) * 512], p[:], AF.Gelu,
                            bias=b0_sb[:, n:n + 1]), s_ge)
                        ck[("pf1", fi % 3)] = s_ge.bump()
                    ck[("w0bu", wbi)] = s_ge.v
                ck[("xt3", xti)] = s_ge.v
                vge_tile = s_ge.v
                if debug and t == 0:
                    rec.wait("gpsimd", s_ge, vge_tile)
                    rec.dma("gpsimd", s_cv[0], dbg_dg,
                            dgb[:].rearrange("p (c n) -> p c n", c=FNC))

                for db in range(4):
                    for sub in range(8):
                        wbi = nw1[0] % 2
                        nw1[0] += 1
                        g1 = ck.get(("w1bu", wbi), None)
                        if g1 is not None:
                            gsem, gval = g1
                            rec.wait("sync", s_f2s if gsem == "f2s" else s_ev2, gval)
                        vw1 = rec.dma("sync", s_w1l[wbi],
                                      w1bu[wbi][:].rearrange("p (c n) -> p c n", c=8),
                                      w1v[:, sub * 8:(sub + 1) * 8, db * 512:(db + 1) * 512])
                        rec.wait("tensor", s_w1l[wbi], vw1)
                        if db == 0 and sub == 0:
                            rec.wait("tensor", s_ge, vge_tile)
                        w13 = w1bu[wbi][:].rearrange("p (c n) -> p c n", c=8)
                        for tc in range(4):
                            pb = p_f2[tc]
                            if sub == 0:
                                rec.wait("tensor", s_ev2, ck.get(("pf2", tc), 0))
                            for c8 in range(8):
                                ncg = sub * 8 + c8
                                l_ap = dgb[:, ncg * 512 + tc * 128: ncg * 512 + (tc + 1) * 128]
                                r_ap = w13[:, c8, :]
                                stf = (sub == 0) and (c8 == 0)
                                spf = (sub == 7) and (c8 == 7)
                                lastmm = (tc == 3) and (c8 == 7) and (sub < 7)
                                if spf:
                                    rec("tensor", lambda e, o=pb, l=l_ap, r=r_ap, s=stf: e.matmul(
                                        o[:], l, r, start=s, stop=True).then_inc(s_f2.h, 1))
                                elif lastmm:
                                    rec("tensor", lambda e, o=pb, l=l_ap, r=r_ap, s=stf: e.matmul(
                                        o[:], l, r, start=s, stop=False).then_inc(s_f2s.h, 1))
                                else:
                                    rec("tensor", lambda e, o=pb, l=l_ap, r=r_ap, s=stf: e.matmul(
                                        o[:], l, r, start=s, stop=False))
                            if sub == 7:
                                s_f2.bump()
                        if sub < 7:
                            ck[("w1bu", wbi)] = ("f2s", s_f2s.bump())
                        else:
                            ck[("w1bu", wbi)] = ("pend", wbi)
                    for tc in range(4):
                        vstop = s_f2.v
                        rec.wait("vector", s_f2, vstop)
                        if db == 0:
                            rec.wait("vector", s_sq, ck.get(("esb", tc), 0))
                        rec.op("vector", lambda e, p=p_f2[tc], tc=tc, db=db: e.scalar_tensor_tensor(
                            esb[tc][:, db * 512:(db + 1) * 512], p[:], 1.0,
                            b1b[:, db * 512:(db + 1) * 512], ALU.mult, ALU.add,
                            accum_out=epart[:, 4 * tc + db:4 * tc + db + 1]), s_ev2)
                        ck[("pf2", tc)] = s_ev2.bump()
                    for key in list(ck):
                        if isinstance(ck.get(key), tuple) and ck[key][0] == "pend":
                            ck[key] = ("ev2", s_ev2.v)
                ck["dgdone"] = s_ev2.v

                for tc in range(4):
                    tcg = t * 4 + tc
                    xbi, vx1 = vx1_t[tc]
                    ybi0 = tcg % 2
                    c0 = slice(4 * tc, 4 * tc + 1)
                    c1 = slice(4 * tc + 1, 4 * tc + 2)
                    c2 = slice(4 * tc + 2, 4 * tc + 3)
                    c3 = slice(4 * tc + 3, 4 * tc + 4)
                    t0 = slice(tc, tc + 1)
                    rec.wait("scalar", s_ev2, ck[("pf2", tc)])
                    if debug:
                        rec.dma("scalar", s_yst[tcg % 2], dbg_e[tcg * 128:(tcg + 1) * 128, :], esb[tc][:])
                    rec.wait("scalar", s_y, ck.get(("sq5", tc), 0))
                    rec.wait("scalar", s_yst[ybi0], ck.get(("yb", ybi0), 0))
                    rec.op("scalar", lambda e, tc=tc, b=ybi0: e.activation(
                        sq5[b][:], esb[tc][:], AF.Square, accum_out=sm_v[:, tc:tc + 1]), s_sq)
                    vsq5 = s_sq.bump()
                    ck[("esb", tc)] = vsq5
                    ck[("sq5w", ybi0)] = vsq5
                    rec.wait("vector", s_sq, vsq5)
                    rec.op("vector", lambda e, a=c0, b=c1: e.tensor_add(epart[:, a], epart[:, a], epart[:, b]))
                    rec.op("vector", lambda e, a=c2, b=c3: e.tensor_add(epart[:, a], epart[:, a], epart[:, b]))
                    rec.op("vector", lambda e, a=c0, b=c2: e.tensor_add(epart[:, a], epart[:, a], epart[:, b]))
                    rec.op("vector", lambda e, a=c0, o=t0: e.tensor_single_scalar(
                        sm_m[:, o], epart[:, a], 1.0 / D, ALU.mult))
                    rec.op("vector", lambda e, o=t0: e.tensor_single_scalar(
                        sm_v[:, o], sm_v[:, o], 1.0 / D, ALU.mult))
                    rec.op("vector", lambda e, o=t0: e.tensor_mul(sm_z[:, o], sm_m[:, o], sm_m[:, o]))
                    rec.op("vector", lambda e, o=t0: e.tensor_sub(sm_v[:, o], sm_v[:, o], sm_z[:, o]))
                    rec.op("vector", lambda e, o=t0: e.tensor_single_scalar(
                        sm_v[:, o], sm_v[:, o], LN_EPS, ALU.add))
                    rec.op("vector", lambda e, o=t0: e.tensor_single_scalar(
                        sm_i[:, o], sm_v[:, o].bitcast(I32), 1, ALU.arith_shift_right))
                    rec.op("vector", lambda e, o=t0: e.tensor_scalar(
                        sm_i[:, o], sm_i[:, o], -1, MAGIC, ALU.mult, ALU.add))
                    rec.op("vector", lambda e, o=t0: e.tensor_copy(sm_y[:, o], sm_i[:, o].bitcast(F32)))
                    for _ in range(3):
                        rec.op("vector", lambda e, o=t0: e.tensor_mul(sm_z[:, o], sm_y[:, o], sm_y[:, o]))
                        rec.op("vector", lambda e, o=t0: e.tensor_mul(sm_z[:, o], sm_z[:, o], sm_v[:, o]))
                        rec.op("vector", lambda e, o=t0: e.tensor_scalar(
                            sm_z[:, o], sm_z[:, o], -0.5, 1.5, ALU.mult, ALU.add))
                        rec.op("vector", lambda e, o=t0: e.tensor_mul(sm_y[:, o], sm_y[:, o], sm_z[:, o]))
                    rec.op("vector", lambda e, o=t0: e.scalar_tensor_tensor(
                        sm_n[:, o], sm_m[:, o], -1.0, sm_y[:, o], ALU.mult, ALU.mult))
                    ybi = tcg % 2
                    rec.wait("vector", s_x1l[xbi], vx1)
                    rec.wait("vector", s_sq, ck.get(("sq5w", ybi), 0))
                    rec.op("vector", lambda e, tc=tc, o=t0, b=ybi: e.tensor_scalar(
                        yb[b][:], esb[tc][:], sm_y[:, o], sm_n[:, o], ALU.mult, ALU.add))
                    rec.op("vector", lambda e, b=ybi, bx=xbi: e.tensor_add(
                        yb[b][:], yb[b][:], x1c[bx][:]), s_y)
                    vy = s_y.bump()
                    ck[("x1c", xbi)] = vy
                    ck[("sq5", tc)] = vy
                    rec.wait("scalar", s_y, vy)
                    ck[("yb", ybi)] = rec.dma("scalar", s_yst[ybi], yv[tcg], yb[ybi][:])
                    if tc in (0, 1):
                        tcgn = t * 4 + tc + 2
                        xbin = tcgn % 2
                        rec.wait("gpsimd", s_y, ck.get(("x1c", xbin), 0))
                        vx1_t[tc + 2] = (xbin, rec.dma("gpsimd", s_x1l[xbin], x1c[xbin][:], x1v[tcgn]))

            rec.wait("scalar", s_yst[0], s_yst[0].v)
            rec.wait("scalar", s_yst[1], s_yst[1].v)
            rec.wait("sync", s_xtl[0], s_xtl[0].v)
            rec.wait("sync", s_xtl[1], s_xtl[1].v)
            rec.wait("sync", s_w0l[0], s_w0l[0].v)
            rec.wait("sync", s_w0l[1], s_w0l[1].v)
            rec.wait("sync", s_w1l[0], s_w1l[0].v)
            rec.wait("sync", s_w1l[1], s_w1l[1].v)
            rec.wait("gpsimd", s_x1l[0], s_x1l[0].v)
            rec.wait("gpsimd", s_x1l[1], s_x1l[1].v)
            rec.flush()

    return nc


_NC_CACHE = {}


def _get_program(seq):
    if seq not in _NC_CACHE:
        _NC_CACHE[seq] = build_program(seq)
    return _NC_CACHE[seq]


def kernel(L0_attn_x, L0_attn_Wq, L0_attn_Wk, L0_attn_Wv, L0_attn_Wo,
           L0_ffn_W0, L0_ffn_bias0, L0_ffn_W1, L0_ffn_bias1):
    x = np.ascontiguousarray(np.asarray(L0_attn_x, dtype=np.float32))
    b, seq, d = x.shape
    assert d == D
    tr = lambda w: np.ascontiguousarray(np.asarray(w, dtype=np.float32).T)
    weights = {
        "wqT": tr(L0_attn_Wq), "wkT": tr(L0_attn_Wk), "wvT": tr(L0_attn_Wv),
        "woT": tr(L0_attn_Wo), "w0T": tr(L0_ffn_W0), "w1T": tr(L0_ffn_W1),
        "b0": np.ascontiguousarray(np.asarray(L0_ffn_bias0, dtype=np.float32)),
        "b1": np.ascontiguousarray(np.asarray(L0_ffn_bias1, dtype=np.float32)),
    }
    nc = _get_program(seq)
    in_maps = [dict(weights, x=np.ascontiguousarray(x[i])) for i in range(b)]
    res = run_bass_kernel_spmd(nc, in_maps, core_ids=list(range(b)))
    out = np.stack([np.asarray(res.results[i]["y"]) for i in range(b)], axis=0)
    return out.astype(np.float32)


# version 08
# speedup vs baseline: 1.0055x; 1.0055x over previous
import math
from contextlib import ExitStack

import numpy as np

import concourse.bass as bass
import concourse.mybir as mybir
from concourse.bass_utils import run_bass_kernel_spmd

F32 = mybir.dt.float32
F16 = mybir.dt.float16
BF16 = mybir.dt.bfloat16
I32 = mybir.dt.int32
AF = mybir.ActivationFunctionType
ALU = mybir.AluOpType
AX = mybir.AxisListType

D = 2048
KC = 16
H = 16
FF = 8192
FNC = 64
LN_EPS = 1e-5
ATTN_EPS = 1e-6
LOG1E4 = math.log(10000.0)
TWO_PI = 2.0 * math.pi
MAGIC = 0x5F3759DF


class Sem:
    def __init__(self, nc, ctx, name):
        self.h = ctx.enter_context(nc.semaphore(name))
        self.v = 0

    def bump(self, k=1):
        self.v += k
        return self.v


class Rec:
    ENGINES = ("sync", "gpsimd", "tensor", "scalar", "vector")

    def __init__(self, nc, chain=None, last=None):
        self.nc = nc
        self.s = {k: [] for k in self.ENGINES}
        self.chain = chain or {}
        self.last = last if last is not None else {}

    def __call__(self, eng, fn):
        self.s[eng].append(fn)

    def op(self, eng, fn, sem=None):
        prev = self.last.get(eng)
        if prev is not None and prev[1] > 0:
            self.wait(eng, prev[0], prev[1])
        if sem is None:
            s = self.chain[eng]
            v = s.bump()
        else:
            s = sem
            v = s.v + 1
        self.s[eng].append(lambda e, f=fn, h=s.h: f(e).then_inc(h, 1))
        self.last[eng] = (s, v)
        return v

    def wait(self, eng, sem, val):
        if val > 0:
            self.s[eng].append(lambda e, s=sem.h, v=val: e.wait_ge(s, v))

    def dma(self, q, sem, out_ap, in_ap):
        v = sem.bump(16)
        self.s[q].append(lambda e, o=out_ap, i=in_ap, s=sem.h: e.dma_start(o, i).then_inc(s, 16))
        return v

    def flush(self):
        with self.nc.Block() as blk:
            for name in self.ENGINES:
                fns = self.s[name]
                if fns:
                    getattr(blk, name)(lambda e, fns=tuple(fns): [f(e) for f in fns])
        self.s = {k: [] for k in self.ENGINES}


def build_program(seq, debug=False):
    assert seq % 1024 == 0
    NCH = seq // 128
    HALF = seq // 2
    HCH = HALF // 128
    HT = HALF // 512
    NQT = seq // 512
    SB = KC // 4

    nc = bass.Bass("TRN2", target_bir_lowering=False)

    x_in = nc.dram_tensor("x", [seq, D], F32, kind="ExternalInput").ap()
    wq = nc.dram_tensor("wqT", [D, D], F32, kind="ExternalInput").ap()
    wk = nc.dram_tensor("wkT", [D, D], F32, kind="ExternalInput").ap()
    wv = nc.dram_tensor("wvT", [D, D], F32, kind="ExternalInput").ap()
    wo = nc.dram_tensor("woT", [D, D], F32, kind="ExternalInput").ap()
    w0 = nc.dram_tensor("w0T", [D, FF], F32, kind="ExternalInput").ap()
    w1 = nc.dram_tensor("w1T", [FF, D], F32, kind="ExternalInput").ap()
    b0_in = nc.dram_tensor("b0", [FF], F32, kind="ExternalInput").ap()
    b1_in = nc.dram_tensor("b1", [D], F32, kind="ExternalInput").ap()
    y_out = nc.dram_tensor("y", [seq, D], F32, kind="ExternalOutput").ap()

    knd = "ExternalOutput" if debug else "Internal"
    if debug:
        dbg_sin = nc.dram_tensor("dbg_sin", [64, seq], F32, kind="ExternalOutput").ap()
        dbg_cos = nc.dram_tensor("dbg_cos", [64, seq], F32, kind="ExternalOutput").ap()
        dbg_wo = nc.dram_tensor("dbg_wo", [seq, D], F32, kind="ExternalOutput").ap()
        dbg_e = nc.dram_tensor("dbg_e", [seq, D], F32, kind="ExternalOutput").ap()
        dbg_dg = nc.dram_tensor("dbg_dg", [128, FNC, 512], BF16, kind="ExternalOutput").ap()
    q_scr = nc.dram_tensor("q_scr", [H, 128, seq], BF16, kind=knd).ap()
    k_scr = nc.dram_tensor("k_scr", [H, 128, seq], BF16, kind=knd).ap()
    v_scr = nc.dram_tensor("v_scr", [H, 128, seq], BF16, kind=knd).ap()
    attn_scr = nc.dram_tensor("attn_scr", [KC, 128, seq], BF16, kind=knd).ap()
    x1_scr = nc.dram_tensor("x1_scr", [seq, D], F32, kind=knd).ap()
    xl2_scr = nc.dram_tensor("xl2_scr", [KC, 128, seq], BF16, kind=knd).ap()
    xlt_scr = nc.dram_tensor("xlt_scr", [seq, D], BF16).ap()
    w0b_scr = nc.dram_tensor("w0b", [D, FF], BF16, kind=knd).ap()
    w1b_scr = nc.dram_tensor("w1b", [FF, D], BF16, kind=knd).ap()

    xv = x_in.rearrange("(c p) d -> c p d", p=128)
    ck = {}

    with ExitStack() as octx:
        def sb(name, shape, dt=F32, ctx=None):
            return (ctx or octx).enter_context(nc.sbuf_tensor(name, shape, dt))

        def pst(name, shape, dt=F32, ctx=None):
            return (ctx or octx).enter_context(nc.psum_tensor(name, shape, dt))

        def S(name, ctx=None):
            return Sem(nc, ctx or octx, name)

        ident = sb("ident", [128, 128], BF16)
        ident64 = sb("ident64", [64, 64], F32)
        ones128 = sb("ones128", [128, 128], BF16)
        ones1f = sb("ones1f", [1, 128], F32)

        b0_sb = sb("b0_sb", [128, FNC], F32)
        b1b = sb("b1b", [128, D], F32)

        s_ld = S("s_ld")
        s_dveS = S("s_dveS")
        s_actS = S("s_actS")
        s_peS = S("s_peS")
        s_xld = [S(f"s_xld{i}") for i in range(2)]
        s_wld = [S(f"s_wld{i}") for i in range(2)]
        s_wdone = S("s_wdone")
        s_mm = S("s_mm")
        s_ev = S("s_ev")
        s_qe = S("s_qe")
        s_qkst = [S(f"s_qkst{i}") for i in range(2)]
        s_sq = S("s_sq")
        s_var = S("s_var")
        s_rstd = S("s_rstd")
        s_xln = S("s_xln")
        s_tp = S("s_tp")
        s_fm = S("s_fm")
        s_cv = [S(f"s_cv{i}") for i in range(2)]
        s_cvo = [S(f"s_cvo{i}") for i in range(2)]
        s_kl = [S(f"s_kl{i}") for i in range(2)]
        s_ql = [S(f"s_ql{i}") for i in range(2)]
        s_vl = [S(f"s_vl{i}") for i in range(2)]
        s_st2 = S("s_st2")
        s_sq2 = S("s_sq2")
        s_s4 = S("s_s4")
        s_av = S("s_av")
        s_dm = S("s_dm")
        s_rc = S("s_rc")
        s_at = S("s_at")
        s_ast = [S(f"s_ast{i}") for i in range(2)]
        s_vt = S("s_vt")
        s_vte = S("s_vte")
        s_hd = S("s_hd")
        s_ain = [S(f"s_ain{i}") for i in range(2)]
        s_wo3 = S("s_wo3")
        s_ev3 = S("s_ev3")
        s_x1 = S("s_x1")
        s_x1st = [S(f"s_x1st{i}") for i in range(2)]
        s_xl2 = S("s_xl2")
        s_x2st = [S(f"s_x2st{i}") for i in range(2)]
        s_xtl = [S(f"s_xtl{i}") for i in range(2)]
        s_x1l = [S(f"s_x1l{i}") for i in range(2)]
        s_w0l = [S(f"s_w0l{i}") for i in range(2)]
        s_w1l = [S(f"s_w1l{i}") for i in range(2)]
        s_f1 = S("s_f1")
        s_ge = S("s_ge")
        s_f2s = S("s_f2s")
        s_f2 = S("s_f2")
        s_ev2 = S("s_ev2")
        s_y = S("s_y")
        s_yst = [S(f"s_yst{i}") for i in range(2)]
        s_wol = S("s_wol")
        s_gps = S("s_gps")
        s_xlt = [S(f"s_xlt{i}") for i in range(2)]
        s_xltl = [S(f"s_xltl{i}") for i in range(2)]
        chain = {"vector": S("s_chv"), "scalar": S("s_chs"), "gpsimd": S("s_chg")}
        last_op = {}

        c12 = octx.enter_context(ExitStack())
        sinf = sb("sinf", [64, seq], F16, c12)
        cosf = sb("cosf", [64, seq], F16, c12)
        cstage = [sb(f"cstage{i}", [128, D], BF16, c12) for i in range(2)]
        masks = sb("masks", [128, 4 * 512], BF16, c12)

        with ExitStack() as c1:
            rec = Rec(nc, chain, last_op)
            b0t = sb("b0t", [64, 128], F32, c1)
            b1row = sb("b1row", [1, D], F32, c1)
            th = sb("th", [64, 1], F32, c1)
            iop = sb("iop", [64, 1], F32, c1)
            posr = sb("posr", [1, seq], F32, c1)
            ubuf = sb("ubuf", [64, seq], F32, c1)
            ibuf = sb("ibuf", [64, seq], I32, c1)
            fbuf = sb("fbuf", [64, seq], F32, c1)
            rsin = sb("rsin", [64, seq], F32, c1)
            rcos = sb("rcos", [64, seq], F32, c1)
            p_s1 = pst("p_s1", [64, 512], F32, c1)
            p_s2 = pst("p_s2", [128, 512], F32, c1)
            p_bt = pst("p_bt", [128, 64], F32, c1)

            g = "gpsimd"
            rec.op(g, lambda e: e.memset(ident[:], 0.0))
            rec.op(g, lambda e: e.affine_select(out=ident[:], in_=ident[:],
                                             compare_op=ALU.not_equal, fill=1.0, base=0,
                                             pattern=[[-1, 128]], channel_multiplier=1))
            rec.op(g, lambda e: e.memset(ident64[:], 0.0))
            rec.op(g, lambda e: e.affine_select(out=ident64[:], in_=ident64[:],
                                             compare_op=ALU.not_equal, fill=1.0, base=0,
                                             pattern=[[-1, 64]], channel_multiplier=1))
            rec.op(g, lambda e: e.memset(ones128[:], 1.0))
            rec.op(g, lambda e: e.memset(ones1f[:], 1.0))
            rec.op(g, lambda e: e.memset(masks[:], 1.0))
            for mi in range(4):
                rec.op(g, lambda e, mi=mi: e.affine_select(
                    out=masks[:, mi * 512:(mi + 1) * 512],
                    in_=masks[:, mi * 512:(mi + 1) * 512],
                    compare_op=ALU.is_ge, fill=0.0, base=-128 * mi,
                    pattern=[[1, 512]], channel_multiplier=-1))
            rec.op(g, lambda e: e.iota(iop[:], [[0, 1]], base=0, channel_multiplier=1,
                                       allow_small_or_imprecise_dtypes=True))
            rec.op(g, lambda e: e.iota(posr[:], [[1, seq]], base=0, channel_multiplier=0,
                                    allow_small_or_imprecise_dtypes=True), s_gps)
            v_gps = s_gps.bump()
            rec.dma(g, s_ld, b0t[:], b0_in.rearrange("(p c) -> p c", p=64))
            vld = rec.dma(g, s_ld, b1row[:], b1_in[None, :])

            rec.wait("vector", s_gps, v_gps)
            rec.wait("tensor", s_gps, v_gps)
            rec.wait("scalar", s_gps, v_gps)
            rec.op("scalar", lambda e: e.activation(th[:], iop[:], AF.Exp,
                                                    scale=-LOG1E4 / 64.0), s_actS)
            v_th = s_actS.bump()

            rec.wait("tensor", s_ld, vld)
            rec("tensor", lambda e: e.transpose(p_bt[:], b0t[:], ident64[:]).then_inc(s_peS.h, 1))
            v_bt = s_peS.bump()
            rec.wait("vector", s_peS, v_bt)
            rec.op("vector", lambda e: e.tensor_copy(b0_sb[:, 0:64], p_bt[:]), s_dveS)
            v_b0c = s_dveS.bump()

            rec.wait("tensor", s_dveS, v_b0c)
            nslab = seq // 512
            for sl in range(nslab + 4):
                if sl < nslab:
                    src = posr[:, sl * 512:(sl + 1) * 512]
                else:
                    src = b1row[:, (sl - nslab) * 512:(sl - nslab + 1) * 512]
                if sl < nslab:
                    rec("tensor", lambda e, s=src: e.matmul(
                        p_s1[:], ones1f[:, 0:64], s, start=True, stop=True).then_inc(s_peS.h, 1))
                else:
                    rec("tensor", lambda e, s=src: e.matmul(
                        p_s2[:], ones1f[:], s, start=True, stop=True).then_inc(s_peS.h, 1))
                vmm = s_peS.bump()
                rec.wait("vector", s_peS, vmm)
                if sl == 0:
                    rec.wait("vector", s_actS, v_th)
                if sl < nslab:
                    sls = slice(sl * 512, (sl + 1) * 512)
                    rec.op("vector", lambda e, s=sls: e.tensor_scalar(
                        ubuf[:, s], p_s1[:], th[:], 1.0 / TWO_PI, ALU.mult, ALU.mult))
                    rec.op("vector", lambda e, s=sls: e.tensor_copy(ibuf[:, s], ubuf[:, s]))
                    rec.op("vector", lambda e, s=sls: e.tensor_copy(fbuf[:, s], ibuf[:, s]))
                    rec.op("vector", lambda e, s=sls: e.tensor_sub(fbuf[:, s], ubuf[:, s], fbuf[:, s]))
                    rec.op("vector", lambda e, s=sls: e.tensor_single_scalar(
                        rsin[:, s], fbuf[:, s], TWO_PI, ALU.mult))
                    rec.op("vector", lambda e, s=sls: e.tensor_single_scalar(
                        ubuf[:, s], ubuf[:, s], 0.25, ALU.add))
                    rec.op("vector", lambda e, s=sls: e.tensor_copy(ibuf[:, s], ubuf[:, s]))
                    rec.op("vector", lambda e, s=sls: e.tensor_copy(fbuf[:, s], ibuf[:, s]))
                    rec.op("vector", lambda e, s=sls: e.tensor_sub(fbuf[:, s], ubuf[:, s], fbuf[:, s]))
                    rec.op("vector", lambda e, s=sls: e.tensor_single_scalar(
                        rcos[:, s], fbuf[:, s], TWO_PI, ALU.mult), s_dveS)
                else:
                    c0 = (sl - nslab) * 512
                    rec.op("vector", lambda e, c0=c0: e.tensor_copy(
                        b1b[:, c0:c0 + 512], p_s2[:]), s_dveS)
                vdv = s_dveS.bump()
                rec.wait("tensor", s_dveS, vdv)
            rec.wait("scalar", s_dveS, s_dveS.v)
            rec.op("scalar", lambda e: e.activation(sinf[:], rsin[:], AF.Sin))
            rec.op("scalar", lambda e: e.activation(cosf[:], rcos[:], AF.Sin), s_actS)
            v_tbl = s_actS.bump()
            if debug:
                rec.wait("gpsimd", s_actS, v_tbl)
                rec.dma("gpsimd", s_ld, dbg_sin, sinf[:])
                rec.dma("gpsimd", s_ld, dbg_cos, cosf[:])
                rec.wait("gpsimd", s_ld, s_ld.v)
            rec.flush()

        if True:
            conv_steps = ([("w0", i, hh) for i in range(KC) for hh in range(4)]
                          + [("w1", i, None) for i in range(FNC)])
            conv_pos = [0]

            def emit_conv(rec, n):
                for _ in range(n):
                    i = conv_pos[0]
                    if i >= len(conv_steps):
                        return
                    conv_pos[0] += 1
                    kind, c, hh = conv_steps[i]
                    sl = i % 2
                    buf = cstage[sl]
                    rec.wait("gpsimd", s_cvo[sl], ck.get(("cvo", sl), 0))
                    if kind == "w0":
                        hs = slice(hh * (FF // 4), (hh + 1) * (FF // 4))
                        vin = rec.dma("gpsimd", s_cv[sl], buf[:], w0[c * 128:(c + 1) * 128, hs])
                        rec.wait("gpsimd", s_cv[sl], vin)
                        vout = rec.dma("gpsimd", s_cvo[sl], w0b_scr[c * 128:(c + 1) * 128, hs], buf[:])
                    else:
                        vin = rec.dma("gpsimd", s_cv[sl], buf[:, 0:D], w1[c * 128:(c + 1) * 128, :])
                        rec.wait("gpsimd", s_cv[sl], vin)
                        vout = rec.dma("gpsimd", s_cvo[sl], w1b_scr[c * 128:(c + 1) * 128, :], buf[:, 0:D])
                    ck[("cvo", sl)] = vout

            with ExitStack() as c2:
                rec = Rec(nc, chain, last_op)
                xbuf = [sb(f"xbuf{i}", [128, D], F32, c2) for i in range(2)]
                sumsq = sb("p1ss", [128, 1], F32, c2)
                xsum = sb("p1s", [128, 1], F32, c2)
                mb = sb("p1m", [128, 1], F32, c2)
                vb = sb("p1v", [128, 1], F32, c2)
                lb = sb("p1l", [128, 1], F32, c2)
                rs = sb("p1r", [128, 1], F32, c2)
                nmr = sb("p1n", [128, 1], F32, c2)
                xlnb = [sb(f"xlnb{i}", [128, D], BF16, c2) for i in range(2)]
                xfm = sb("xfm", [128, KC * HALF], BF16, c2)
                wsb = [sb(f"wsb{i}", [128, KC * 512], BF16, c2) for i in range(2)]
                rt1 = sb("rt1", [64, 512], F16, c2)
                rt2 = sb("rt2", [64, 512], F16, c2)
                qlo16 = [sb(f"qlo16{i}", [64, 512], F16, c2) for i in range(2)]
                qhi16 = [sb(f"qhi16{i}", [64, 512], F16, c2) for i in range(2)]
                roblo = [sb(f"roblo{i}", [64, HALF], BF16, c2) for i in range(2)]
                robhi = [sb(f"robhi{i}", [64, HALF], BF16, c2) for i in range(2)]
                p_pr = [pst(f"p_pr{i}", [128, 512], F32, c2) for i in range(4)]
                p_tp = [pst(f"p_tp{i}", [128, 1024], BF16, c2) for i in range(4)]

                wT = {"q": wq, "k": wk, "v": wv}
                scr = {"q": q_scr, "k": k_scr, "v": v_scr}
                xfm3 = xfm[:].rearrange("p (k t) -> p k t", k=KC)
                gi_ctr = [0]
                tp_ctr = [0]

                def wait_ck(eng, key):
                    g = ck.get(key)
                    if g is not None:
                        rec.wait(eng, g[0], g[1])

                def emit_tp_chunk(c, src, gate_sem, gate_val):
                    rec.wait("tensor", gate_sem, gate_val)
                    for grp in range(2):
                        ti = tp_ctr[0]
                        tp_ctr[0] += 1
                        pt = p_tp[ti % 4]
                        rec.wait("tensor", s_fm, ck.get(("ptp", ti % 4), 0))
                        for j in range(8):
                            kcc = grp * 8 + j
                            if j < 7:
                                rec("tensor", lambda e, x=src, k=kcc, p=pt, j=j: e.transpose(
                                    p[:, j * 128:(j + 1) * 128],
                                    x[:, k * 128:(k + 1) * 128], ident[:]))
                            else:
                                rec("tensor", lambda e, x=src, k=kcc, p=pt, j=j: e.transpose(
                                    p[:, j * 128:(j + 1) * 128],
                                    x[:, k * 128:(k + 1) * 128],
                                    ident[:]).then_inc(s_tp.h, 1))
                        vtp = s_tp.bump()
                        rec.wait("scalar", s_tp, vtp)
                        dst = xfm3[:, grp * 8:(grp + 1) * 8, c * 128:(c + 1) * 128]
                        srcp = pt[:].rearrange("p (j t) -> p j t", j=8)
                        rec.op("scalar", lambda e, d=dst, s=srcp: e.activation(
                            d, s, AF.Identity), s_fm)
                        ck[("ptp", ti % 4)] = s_fm.bump()
                    return s_tp.v

                def emit_ln_chunk(gc, to_scratch):
                    c = gc % HCH
                    bi = gc % 2
                    rec.wait("sync", s_xln, ck.get(("xbuf", bi), 0))
                    vx = rec.dma("sync", s_xld[bi], xbuf[bi][:], xv[gc])
                    rec.wait("scalar", s_xld[bi], vx)
                    rec.wait("scalar", s_var, ck.get("p1var", 0))
                    wait_ck("scalar", ("xlnb", bi))
                    rec.op("scalar", lambda e, b=bi: e.activation(
                        xlnb[b][:], xbuf[b][:], AF.Square, accum_out=sumsq[:]), s_sq)
                    vsq = s_sq.bump()
                    rec.wait("vector", s_xld[bi], vx)
                    rec.wait("vector", s_rstd, ck.get("p1rstd", 0))
                    rec.op("vector", lambda e, b=bi: e.reduce_sum(xsum[:], xbuf[b][:], axis=AX.X))
                    rec.wait("vector", s_sq, vsq)
                    rec.op("vector", lambda e: e.tensor_single_scalar(mb[:], xsum[:], 1.0 / D, ALU.mult))
                    rec.op("vector", lambda e: e.tensor_single_scalar(vb[:], sumsq[:], 1.0 / D, ALU.mult))
                    rec.op("vector", lambda e: e.tensor_mul(xsum[:], mb[:], mb[:]))
                    rec.op("vector", lambda e: e.tensor_sub(vb[:], vb[:], xsum[:]))
                    rec.op("vector", lambda e: e.tensor_single_scalar(
                        vb[:], vb[:], LN_EPS, ALU.add), s_var)
                    ck["p1var"] = s_var.bump()
                    rec.wait("scalar", s_var, ck["p1var"])
                    rec.op("scalar", lambda e: e.activation(lb[:], vb[:], AF.Ln))
                    rec.op("scalar", lambda e: e.activation(
                        rs[:], lb[:], AF.Exp, scale=-0.5), s_rstd)
                    ck["p1rstd"] = s_rstd.bump()
                    rec.wait("vector", s_rstd, ck["p1rstd"])
                    wait_ck("vector", ("xlnb", bi))
                    rec.op("vector", lambda e: e.scalar_tensor_tensor(
                        nmr[:], mb[:], -1.0, rs[:], ALU.mult, ALU.mult))
                    rec.op("vector", lambda e, b=bi: e.tensor_scalar(
                        xlnb[b][:], xbuf[b][:], rs[:], nmr[:], ALU.mult, ALU.add), s_xln)
                    ck[("xbuf", bi)] = s_xln.bump()
                    if to_scratch:
                        rec.wait("gpsimd", s_xln, ck[("xbuf", bi)])
                        vst = rec.dma("gpsimd", s_xlt[bi], xlt_scr[gc * 128:(gc + 1) * 128, :], xlnb[bi][:])
                        ck[("xltst", gc)] = vst
                        ck[("xlnb", bi)] = (s_xlt[bi], vst)
                    else:
                        vtp = emit_tp_chunk(c, xlnb[bi][:], s_xln, ck[("xbuf", bi)])
                        ck[("xlnb", bi)] = (s_tp, vtp)

                def emit_lite_chunk(gc):
                    c = gc % HCH
                    bi = gc % 2
                    wait_ck("sync", ("xlnb", bi))
                    rec.wait("sync", s_xlt[bi], ck[("xltst", gc)])
                    vld = rec.dma("sync", s_xltl[bi], xlnb[bi][:], xlt_scr[gc * 128:(gc + 1) * 128, :])
                    vtp = emit_tp_chunk(c, xlnb[bi][:], s_xltl[bi], vld)
                    ck[("xlnb", bi)] = (s_tp, vtp)

                for hf in range(2):
                    if hf == 0:
                        for c in range(HCH):
                            emit_ln_chunk(c, False)
                    else:
                        for c in range(HCH):
                            emit_lite_chunk(HCH + c)
                    ck[("fmhalf", hf)] = s_fm.v

                    for sbi in range(3 * SB):
                        proj = "qkv"[sbi // SB]
                        blkc = sbi % SB
                        gsb = hf * 3 * SB + sbi
                        wbi = gsb % 2
                        rec.wait("gpsimd", s_qe, ck.get(("wsb", wbi), 0))
                        wsrc = wT[proj].rearrange("(k p) n -> p k n", p=128)[:, :, blkc * 512:(blkc + 1) * 512]
                        vw = rec.dma("gpsimd", s_wld[wbi],
                                     wsb[wbi][:].rearrange("p (k n) -> p k n", k=KC), wsrc)
                        rec.wait("tensor", s_wld[wbi], vw)
                        if sbi == 0:
                            rec.wait("tensor", s_fm, ck[("fmhalf", hf)])
                        wv3 = wsb[wbi][:].rearrange("p (k n) -> p k n", k=KC)
                        for dqc in range(4):
                            dqg = blkc * 4 + dqc
                            slot = dqg % 2
                            for t in range(HT):
                                gi = gi_ctr[0]
                                gi_ctr[0] += 1
                                pb = p_pr[gi % 4]
                                rec.wait("tensor", s_qe, ck.get(("ppr", gi % 4), 0))
                                for kcc in range(KC):
                                    l_ap = wv3[:, kcc, dqc * 128:(dqc + 1) * 128]
                                    r_ap = xfm3[:, kcc, t * 512:(t + 1) * 512]
                                    stf = kcc == 0
                                    if kcc == KC - 1:
                                        rec("tensor", lambda e, o=pb, l=l_ap, r=r_ap, s=stf: e.matmul(
                                            o[:], l, r, start=s, stop=True).then_inc(s_mm.h, 1))
                                    else:
                                        rec("tensor", lambda e, o=pb, l=l_ap, r=r_ap, s=stf: e.matmul(
                                            o[:], l, r, start=s, stop=False))
                                vmm = s_mm.bump()
                                cols = slice(t * 512, (t + 1) * 512)
                                gcols = slice(hf * HALF + t * 512, hf * HALF + (t + 1) * 512)
                                rlo, rhi = roblo[slot], robhi[slot]
                                g2 = gi % 2
                                rec.wait("scalar", s_mm, vmm)
                                if proj in ("q", "k"):
                                    rec.wait("scalar", s_ev, ck.get(("q16", g2), 0))
                                    rec.op("scalar", lambda e, p=pb, o=qlo16[g2]: e.activation(
                                        o[:], p[0:64, :], AF.Identity))
                                    rec.op("scalar", lambda e, p=pb, o=qhi16[g2]: e.activation(
                                        o[:], p[64:128, :], AF.Identity), s_qe)
                                    vqe = s_qe.bump()
                                    ck[("ppr", gi % 4)] = vqe
                                    rec.wait("vector", s_qe, vqe)
                                    if t == 0:
                                        rec.wait("vector", s_qkst[slot], ck.get(("rob", slot), 0))
                                    rec.op("vector", lambda e, q=qlo16[g2], g=gcols: e.tensor_mul(
                                        rt1[:], q[:], cosf[:, g]))
                                    rec.op("vector", lambda e, q=qhi16[g2], g=gcols: e.tensor_mul(
                                        rt2[:], q[:], sinf[:, g]))
                                    rec.op("vector", lambda e, r=rlo, c=cols: e.tensor_sub(
                                        r[:, c], rt1[:], rt2[:]))
                                    rec.op("vector", lambda e, q=qlo16[g2], g=gcols: e.tensor_mul(
                                        rt1[:], q[:], sinf[:, g]))
                                    rec.op("vector", lambda e, q=qhi16[g2], g=gcols: e.tensor_mul(
                                        rt2[:], q[:], cosf[:, g]))
                                    rec.op("vector", lambda e, r=rhi, c=cols: e.tensor_add(
                                        r[:, c], rt1[:], rt2[:]), s_ev)
                                    ck[("q16", g2)] = s_ev.bump()
                                    if t == HT - 1:
                                        rec.wait("scalar", s_ev, ck[("q16", g2)])
                                        hc = slice(hf * HALF, (hf + 1) * HALF)
                                        rec.dma("scalar", s_qkst[slot],
                                                scr[proj][dqg, 0:64, hc], rlo[:])
                                        ck[("rob", slot)] = rec.dma(
                                            "scalar", s_qkst[slot],
                                            scr[proj][dqg, 64:128, hc], rhi[:])
                                else:
                                    if t == 0:
                                        rec.wait("scalar", s_qkst[slot], ck.get(("rob", slot), 0))
                                    rec.op("scalar", lambda e, p=pb, r=rlo, c=cols: e.activation(
                                        r[:, c], p[0:64, :], AF.Identity))
                                    rec.op("scalar", lambda e, p=pb, r=rhi, c=cols: e.activation(
                                        r[:, c], p[64:128, :], AF.Identity), s_qe)
                                    ck[("ppr", gi % 4)] = s_qe.bump()
                                    if t == HT - 1:
                                        hc = slice(hf * HALF, (hf + 1) * HALF)
                                        rec.dma("scalar", s_qkst[slot],
                                                scr[proj][dqg, 0:64, hc], rlo[:])
                                        ck[("rob", slot)] = rec.dma(
                                            "scalar", s_qkst[slot],
                                            scr[proj][dqg, 64:128, hc], rhi[:])
                        ck[("wsb", wbi)] = s_qe.v
                        if hf == 0 and sbi >= 2 and 2 * (sbi - 2) + 1 < HCH:
                            emit_ln_chunk(HCH + 2 * (sbi - 2), True)
                            emit_ln_chunk(HCH + 2 * (sbi - 2) + 1, True)
                    emit_conv(rec, 25)

                rec.wait("scalar", s_qkst[0], s_qkst[0].v)
                rec.wait("scalar", s_qkst[1], s_qkst[1].v)
                rec.wait("sync", s_xld[0], s_xld[0].v)
                rec.wait("sync", s_xld[1], s_xld[1].v)
                rec.wait("gpsimd", s_wld[0], s_wld[0].v)
                rec.wait("gpsimd", s_wld[1], s_wld[1].v)
                rec.flush()

            with ExitStack() as c3:
                rec = Rec(nc, chain, last_op)
                kb = [sb(f"kb{i}", [128, seq], BF16, c3) for i in range(2)]
                qb = [sb(f"qb{i}", [128, seq], BF16, c3) for i in range(2)]
                vbf = [sb(f"vb{i}", [128, seq], BF16, c3) for i in range(2)]
                vtm = [sb(f"vtm{i}", [128, seq], BF16, c3) for i in range(2)]
                s2b = [sb(f"s2b{i}", [128, 512], BF16, c3) for i in range(7)]
                s2m = sb("s2m", [128, 512], BF16, c3)
                s4b = [sb(f"s4b{i}", [128, 512], BF16, c3) for i in range(7)]
                dmb = [sb(f"dmb{i}", [128, 512], F32, c3) for i in range(2)]
                rcb = [sb(f"rcb{i}", [128, 512], F32, c3) for i in range(2)]
                atb = [sb(f"atb{i}", [128, 512], BF16, c3) for i in range(2)]
                p_st = [pst(f"p_st{i}", [128, 512], F32, c3) for i in range(4)]
                p_o = [pst(f"p_o{i}", [128, 512], F32, c3) for i in range(2)]
                p_d = [pst(f"p_d{i}", [128, 512], F32, c3) for i in range(1)]
                p_vt = pst("p_vt", [128, 1024], BF16, c3)

                emit_conv(rec, len(conv_steps))
                n2 = [0]
                nqt = [0]

                def emit_chunk(h, bi, qt, jc):
                    i = n2[0] + jc
                    pb = p_st[i % 4]
                    rec.wait("tensor", s_sq2, ck.get(("pst", i % 4), 0))
                    rec("tensor", lambda e, p=pb, b=bi, jc=jc, q0=qt * 512: e.matmul(
                        p[:], kb[b][:, jc * 128:(jc + 1) * 128], qb[b][:, q0:q0 + 512],
                        start=True, stop=True).then_inc(s_st2.h, 1))
                    vst = s_st2.bump()
                    rec.wait("scalar", s_st2, vst)
                    rec.wait("scalar", s_s4, ck.get(("s2b", i % 7), 0))
                    rec("scalar", lambda e, p=pb, o=s2b[i % 7]: e.activation(
                        o[:], p[:], AF.Square).then_inc(s_sq2.h, 1))
                    rec.last["scalar"] = (s_sq2, s_sq2.v + 1)
                    vs2 = s_sq2.bump()
                    ck[("pst", i % 4)] = vs2
                    rec.wait("vector", s_sq2, vs2)
                    rec.wait("vector", s_av, ck.get(("s4b", i % 7), 0))
                    dmi = jc - 4 * qt
                    if dmi >= 0:
                        rec.op("vector", lambda e, s=s2b[i % 7], m=dmi: e.tensor_mul(
                            s2m[:], s[:], masks[:, m * 512:(m + 1) * 512]))
                        rec.op("vector", lambda e, s=s2b[i % 7], o=s4b[i % 7]: e.tensor_mul(
                            o[:], s[:], s2m[:]), s_s4)
                        ck[("s2b", i % 7)] = s_s4.bump()
                    else:
                        rec("vector", lambda e, s=s2b[i % 7], o=s4b[i % 7]: e.tensor_mul(
                            o[:], s[:], s[:]).then_inc(s_s4.h, 1))
                        rec.last["vector"] = (s_s4, s_s4.v + 1)
                        ck[("s2b", i % 7)] = s_s4.bump()
                    return ck[("s2b", i % 7)], i

                NG_VT = seq // 1024

                def emit_vt_group(h, grp, first):
                    b = h % 2
                    if first:
                        rec.wait("tensor", s_vl[b], ck[("vload", h)])
                    rec.wait("tensor", s_vte, ck.get("pvt", 0))
                    for j in range(8):
                        cc = grp * 8 + j
                        if j < 7:
                            rec("tensor", lambda e, b=b, c=cc, j=j: e.transpose(
                                p_vt[:, j * 128:(j + 1) * 128],
                                vbf[b][:, c * 128:(c + 1) * 128], ident[:]))
                        else:
                            rec("tensor", lambda e, b=b, c=cc, j=j: e.transpose(
                                p_vt[:, j * 128:(j + 1) * 128],
                                vbf[b][:, c * 128:(c + 1) * 128], ident[:]).then_inc(s_vt.h, 1))
                    vvt = s_vt.bump()
                    rec.wait("scalar", s_vt, vvt)
                    dst = vtm[b][:].rearrange("p (g j t) -> p g j t", g=NG_VT, j=8)[:, grp]
                    srcp = p_vt[:].rearrange("p (j t) -> p j t", j=8)
                    rec.op("scalar", lambda e, d=dst, s=srcp: e.activation(
                        d, s, AF.Identity), s_vte)
                    ck["pvt"] = s_vte.bump()
                    ck[("vte_head", h)] = ck["pvt"]

                def rec_head_loads(hh):
                    b = hh % 2
                    rec.wait("sync", s_at, ck.get(("kqv", b), 0))
                    ck[("kload", hh)] = rec.dma("sync", s_kl[b], kb[b][:], k_scr[hh])
                    ck[("qload", hh)] = rec.dma("sync", s_ql[b], qb[b][:], q_scr[hh])
                    ck[("vload", hh)] = rec.dma("sync", s_vl[b], vbf[b][:], v_scr[hh])

                rec_head_loads(0)
                for h in range(H):
                    bi = h % 2
                    if h + 1 < H:
                        rec_head_loads(h + 1)
                    if ("vte_head", h) not in ck:
                        for grp in range(NG_VT):
                            emit_vt_group(h, grp, grp == 0)
                    vte_head = ck[("vte_head", h)]

                    rec.wait("tensor", s_kl[bi], ck[("kload", h)])
                    rec.wait("tensor", s_ql[bi], ck[("qload", h)])
                    for qt in range(NQT):
                        njc = 4 * qt + 4
                        poi = nqt[0] % 2
                        LAG = 6
                        vs4 = {}
                        for j in range(min(LAG, njc)):
                            vs4[j] = emit_chunk(h, bi, qt, j)
                        for jc in range(njc):
                            v4, i4 = vs4[jc]
                            rec.wait("tensor", s_s4, v4)
                            if jc == 0:
                                rec.wait("tensor", s_vte, vte_head)
                                rec.wait("tensor", s_at, ck.get(("po", poi), 0))
                                rec.wait("tensor", s_dm, ck.get("pd2", 0))
                            last = jc == njc - 1
                            rec("tensor", lambda e, po=p_o[poi], jc=jc, s4=s4b[i4 % 7],
                                vt=vtm[bi], st=(jc == 0), sp=last: e.matmul(
                                po[:], vt[:, jc * 128:(jc + 1) * 128], s4[:], start=st, stop=sp))
                            rec("tensor", lambda e, pd=p_d[0], s4=s4b[i4 % 7],
                                st=(jc == 0), sp=last: e.matmul(
                                pd[:], ones128[:], s4[:], start=st, stop=sp).then_inc(s_av.h, 1))
                            ck[("s4b", i4 % 5)] = s_av.bump()
                            if jc + LAG < njc:
                                vs4[jc + LAG] = emit_chunk(h, bi, qt, jc + LAG)
                        if h + 1 < H and 2 <= qt < 2 + NG_VT and NQT > 2 + NG_VT:
                            emit_vt_group(h + 1, qt - 2, qt == 2)
                        n2[0] += njc
                        vav = s_av.v
                        rec.wait("vector", s_av, vav)
                        rec.wait("vector", s_rc, ck.get(("dmb", poi), 0))
                        rec.op("vector", lambda e, p=p_d[0], o=dmb[poi]: e.tensor_single_scalar(
                            o[:], p[:], ATTN_EPS, ALU.max), s_dm)
                        vdm = s_dm.bump()
                        ck["pd2"] = vdm
                        rec.wait("scalar", s_dm, vdm)
                        rec.wait("scalar", s_at, ck.get(("rcb", poi), 0))
                        rec.op("scalar", lambda e, i=poi: e.activation(rcb[i][:], dmb[i][:], AF.Ln))
                        rec.op("scalar", lambda e, i=poi: e.activation(
                            rcb[i][:], rcb[i][:], AF.Exp, scale=-1.0), s_rc)
                        ck[("dmb", poi)] = s_rc.bump()
                        rec.wait("vector", s_rc, ck[("dmb", poi)])
                        rec.wait("vector", s_ast[poi], ck.get(("atb", poi), 0))
                        rec.op("vector", lambda e, i=poi: e.tensor_mul(
                            atb[i][:], p_o[i][:], rcb[i][:]), s_at)
                        vat = s_at.bump()
                        ck[("rcb", poi)] = vat
                        ck[("po", poi)] = vat
                        rec.wait("scalar", s_at, vat)
                        ck[("atb", poi)] = rec.dma("scalar", s_ast[poi],
                                                   attn_scr[h, :, qt * 512:(qt + 1) * 512], atb[poi][:])
                        nqt[0] += 1
                    ck[("kqv", bi)] = s_at.v

                rec.wait("scalar", s_ast[0], s_ast[0].v)
                rec.wait("scalar", s_ast[1], s_ast[1].v)
                rec.wait("gpsimd", s_cvo[0], s_cvo[0].v)
                rec.wait("gpsimd", s_cvo[1], s_cvo[1].v)
                rec.wait("sync", s_kl[0], s_kl[0].v)
                rec.wait("sync", s_kl[1], s_kl[1].v)
                rec.wait("sync", s_ql[0], s_ql[0].v)
                rec.wait("sync", s_ql[1], s_ql[1].v)
                rec.wait("sync", s_vl[0], s_vl[0].v)
                rec.wait("sync", s_vl[1], s_vl[1].v)
                rec.flush()

        c12.close()

        with ExitStack() as c4:
            rec = Rec(nc, chain, last_op)
            wot = sb("wot", [128, KC * D], BF16, c4)
            ain = [sb(f"ain{i}", [128, KC * 512], BF16, c4) for i in range(2)]
            xb3 = [sb(f"xb3{i}", [128, D], F32, c4) for i in range(2)]
            wsb3 = [sb(f"wsb3{i}", [128, D], F32, c4) for i in range(2)]
            qsum2 = sb("qsum2", [128, 8], F32, c4)
            sm3 = sb("sm3", [128, 12], F32, c4)
            lnb = sb("lnb", [128, D], F32, c4)
            sq3 = sb("sq3d", [128, D], F32, c4)
            x1b = [sb(f"x1b{i}", [128, D], F32, c4) for i in range(2)]
            xl2t = [sb(f"xl2t{i}", [128, D], BF16, c4) for i in range(2)]
            xl2f2 = [sb(f"xl2f2{i}", [128, KC * 128], BF16, c4) for i in range(2)]
            p_q3 = [pst(f"p_q3{i}", [128, 512], F32, c4) for i in range(6)]
            p_t3 = [pst(f"p_t3{i}", [128, 1024], BF16, c4) for i in range(2)]

            vwo = rec.dma("gpsimd", s_wol, wot[:].rearrange("p (k n) -> p k n", k=KC),
                          wo.rearrange("(k p) n -> p k n", p=128))
            wot3 = wot[:].rearrange("p (k n) -> p k n", k=KC)
            nq3 = [0]
            tp3 = [0]

            def emit_tp3(tce):
                b = tce % 2
                rec.wait("tensor", s_xl2, ck[("xl2done", tce)])
                for grp in range(2):
                    ti = tp3[0]
                    tp3[0] += 1
                    pt = p_t3[ti % 2]
                    rec.wait("tensor", s_fm, ck.get(("pt3", ti % 2), 0))
                    for j in range(8):
                        kcc = grp * 8 + j
                        if j < 7:
                            rec("tensor", lambda e, b=b, k=kcc, p=pt, j=j: e.transpose(
                                p[:, j * 128:(j + 1) * 128],
                                xl2t[b][:, k * 128:(k + 1) * 128], ident[:]))
                        else:
                            rec("tensor", lambda e, b=b, k=kcc, p=pt, j=j: e.transpose(
                                p[:, j * 128:(j + 1) * 128],
                                xl2t[b][:, k * 128:(k + 1) * 128], ident[:]).then_inc(s_tp.h, 1))
                    vtp = s_tp.bump()
                    if grp == 1:
                        ck[("xl2t", b)] = vtp
                    rec.wait("scalar", s_tp, vtp)
                    if grp == 0:
                        rec.wait("scalar", s_x2st[b], ck.get(("xl2f2", b), 0))
                    dst = xl2f2[b][:].rearrange("p (k t) -> p k t", k=KC)[:, grp * 8:(grp + 1) * 8, :]
                    srcp = pt[:].rearrange("p (j t) -> p j t", j=8)
                    rec.op("scalar", lambda e, d=dst, s=srcp: e.activation(
                        d, s, AF.Identity), s_fm)
                    ck[("pt3", ti % 2)] = s_fm.bump()
                rec.wait("scalar", s_fm, ck[("pt3", (tp3[0] - 1) % 2)])
                ck[("xl2f2", b)] = rec.dma(
                    "scalar", s_x2st[b],
                    xl2_scr.rearrange("k p n -> p k n")[:, :, tce * 128:(tce + 1) * 128],
                    xl2f2[b][:].rearrange("p (k t) -> p k t", k=KC))
            for gidx in range(seq // 512):
                gbi = gidx % 2
                rec.wait("sync", s_ev3, ck.get(("ain", gbi), 0))
                va = rec.dma("sync", s_ain[gbi], ain[gbi][:].rearrange("p (k t) -> p k t", k=KC),
                             attn_scr.rearrange("k p n -> p k n")[:, :, gidx * 512:(gidx + 1) * 512])
                ain3 = ain[gbi][:].rearrange("p (k t) -> p k t", k=KC)
                for tcl in range(4):
                    tc = gidx * 4 + tcl
                    bi = tc % 2
                    Q0 = qsum2[:, 4 * bi + 0:4 * bi + 1]
                    Q1 = qsum2[:, 4 * bi + 1:4 * bi + 2]
                    Q2 = qsum2[:, 4 * bi + 2:4 * bi + 3]
                    Q3 = qsum2[:, 4 * bi + 3:4 * bi + 4]
                    SS3 = sm3[:, 6 * bi + 0:6 * bi + 1]
                    M3 = sm3[:, 6 * bi + 1:6 * bi + 2]
                    V3 = sm3[:, 6 * bi + 2:6 * bi + 3]
                    L3 = sm3[:, 6 * bi + 3:6 * bi + 4]
                    RS3 = sm3[:, 6 * bi + 4:6 * bi + 5]
                    NM3 = sm3[:, 6 * bi + 5:6 * bi + 6]
                    rec.wait("gpsimd", s_x1, ck.get(("xb3", bi), 0))
                    vx = rec.dma("gpsimd", s_xld[bi], xb3[bi][:], xv[tc])
                    rec.wait("tensor", s_ain[gbi], va)
                    if tc == 0:
                        rec.wait("tensor", s_wol, vwo)
                    for qd in range(4):
                        qi = nq3[0]
                        nq3[0] += 1
                        pqb = p_q3[qi % 6]
                        rec.wait("tensor", s_ev3, ck.get(("pq3", qi % 6), 0))
                        for kcc in range(KC):
                            l_ap = ain3[:, kcc, tcl * 128:(tcl + 1) * 128]
                            r_ap = wot3[:, kcc, qd * 512:(qd + 1) * 512]
                            stf = kcc == 0
                            if kcc == KC - 1:
                                rec("tensor", lambda e, o=pqb, l=l_ap, r=r_ap, s=stf: e.matmul(
                                    o[:], l, r, start=s, stop=True).then_inc(s_wo3.h, 1))
                            else:
                                rec("tensor", lambda e, o=pqb, l=l_ap, r=r_ap, s=stf: e.matmul(
                                    o[:], l, r, start=s, stop=False))
                        vq3 = s_wo3.bump()
                        rec.wait("scalar", s_wo3, vq3)
                        if qd == 0:
                            rec.wait("scalar", s_xl2, ck.get(("p3adve", bi), 0))
                        rec.op("scalar", lambda e, p=pqb, b=bi, qd=qd: e.activation(
                            wsb3[b][:, qd * 512:(qd + 1) * 512], p[:], AF.Identity,
                            accum_out=qsum2[:, 4 * bi + qd:4 * bi + qd + 1]), s_ev3)
                        ck[("pq3", qi % 6)] = s_ev3.bump()
                    vev = s_ev3.v
                    rec.wait("scalar", s_ev3, vev)
                    if debug:
                        rec.dma("scalar", s_x1st[bi], dbg_wo[tc * 128:(tc + 1) * 128, :], wsb3[bi][:])
                    rec.op("scalar", lambda e, b=bi: e.activation(
                        sq3[:], wsb3[b][:], AF.Square, accum_out=SS3), s_sq)
                    vsq = s_sq.bump()
                    rec.wait("vector", s_sq, vsq)
                    rec.op("vector", lambda e: e.tensor_add(Q0, Q0, Q1))
                    rec.op("vector", lambda e: e.tensor_add(Q2, Q2, Q3))
                    rec.op("vector", lambda e: e.tensor_add(Q0, Q0, Q2))
                    rec.op("vector", lambda e: e.tensor_single_scalar(M3, Q0, 1.0 / D, ALU.mult))
                    rec.op("vector", lambda e: e.tensor_single_scalar(V3, SS3, 1.0 / D, ALU.mult))
                    rec.op("vector", lambda e: e.tensor_mul(Q1, M3, M3))
                    rec.op("vector", lambda e: e.tensor_sub(V3, V3, Q1))
                    rec.op("vector", lambda e: e.tensor_single_scalar(
                        V3, V3, LN_EPS, ALU.add), s_var)
                    vv3 = s_var.bump()
                    rec.wait("scalar", s_var, vv3)
                    rec.op("scalar", lambda e: e.activation(L3, V3, AF.Ln))
                    rec.op("scalar", lambda e: e.activation(
                        RS3, L3, AF.Exp, scale=-0.5), s_rstd)
                    vr3 = s_rstd.bump()
                    rec.wait("vector", s_rstd, vr3)
                    rec.wait("vector", s_xld[bi], vx)
                    rec.wait("vector", s_x1st[bi], ck.get(("x1b", bi), 0))
                    rec.op("vector", lambda e: e.scalar_tensor_tensor(
                        NM3, M3, -1.0, RS3, ALU.mult, ALU.mult))
                    rec.op("vector", lambda e, b=bi: e.tensor_scalar(
                        lnb[:], wsb3[b][:], RS3, NM3, ALU.mult, ALU.add))
                    rec.op("vector", lambda e, b=bi: e.tensor_add(
                        x1b[b][:], lnb[:], xb3[b][:]), s_x1)
                    ck[("xb3", bi)] = s_x1.bump()
                    rec.wait("scalar", s_x1, ck[("xb3", bi)])
                    ck[("x1b", bi)] = rec.dma("scalar", s_x1st[bi],
                                              x1_scr[tc * 128:(tc + 1) * 128, :], x1b[bi][:])
                    rec.op("scalar", lambda e, b=bi: e.activation(
                        sq3[:], x1b[b][:], AF.Square, accum_out=SS3), s_sq)
                    vsq2 = s_sq.bump()
                    rec.wait("vector", s_sq, vsq2)
                    rec.op("vector", lambda e, b=bi: e.reduce_sum(Q0, x1b[b][:], axis=AX.X))
                    rec.op("vector", lambda e: e.tensor_single_scalar(M3, Q0, 1.0 / D, ALU.mult))
                    rec.op("vector", lambda e: e.tensor_single_scalar(V3, SS3, 1.0 / D, ALU.mult))
                    rec.op("vector", lambda e: e.tensor_mul(Q1, M3, M3))
                    rec.op("vector", lambda e: e.tensor_sub(V3, V3, Q1))
                    rec.op("vector", lambda e: e.tensor_single_scalar(
                        V3, V3, LN_EPS, ALU.add), s_var)
                    vv4 = s_var.bump()
                    rec.wait("scalar", s_var, vv4)
                    rec.op("scalar", lambda e: e.activation(L3, V3, AF.Ln))
                    rec.op("scalar", lambda e: e.activation(
                        RS3, L3, AF.Exp, scale=-0.5), s_rstd)
                    vr4 = s_rstd.bump()
                    rec.wait("vector", s_rstd, vr4)
                    rec.wait("vector", s_tp, ck.get(("xl2t", bi), 0))
                    rec.op("vector", lambda e: e.scalar_tensor_tensor(
                        NM3, M3, -1.0, RS3, ALU.mult, ALU.mult))
                    rec.op("vector", lambda e, b=bi: e.tensor_scalar(
                        xl2t[b][:], x1b[b][:], RS3, NM3, ALU.mult, ALU.add), s_xl2)
                    ck[("p3adve", bi)] = s_xl2.bump()
                    ck[("xl2done", tc)] = ck[("p3adve", bi)]
                    if tc >= 1:
                        emit_tp3(tc - 1)
                    if tc == NCH - 1:
                        emit_tp3(tc)
                ck[("ain", gbi)] = s_ev3.v

            rec.wait("scalar", s_x1st[0], s_x1st[0].v)
            rec.wait("scalar", s_x1st[1], s_x1st[1].v)
            rec.wait("scalar", s_x2st[0], s_x2st[0].v)
            rec.wait("scalar", s_x2st[1], s_x2st[1].v)
            rec.wait("sync", s_ain[0], s_ain[0].v)
            rec.wait("sync", s_ain[1], s_ain[1].v)
            rec.wait("gpsimd", s_xld[0], s_xld[0].v)
            rec.wait("gpsimd", s_xld[1], s_xld[1].v)
            rec.wait("gpsimd", s_wol, s_wol.v)
            rec.flush()

        with ExitStack() as c5:
            rec = Rec(nc, chain, last_op)
            dgb = sb("dgb", [128, FNC * 512], BF16, c5)
            xt3b = [sb(f"xt3{i}", [128, KC * 512], BF16, c5) for i in range(2)]
            w0bu = [sb(f"w0bu{i}", [128, KC * 256], BF16, c5) for i in range(2)]
            w1bu = [sb(f"w1bu{i}", [128, 8 * 512], BF16, c5) for i in range(2)]
            x1c = [sb(f"x1c{i}", [128, D], F32, c5) for i in range(2)]
            esb = [sb(f"esb{i}", [128, D], F32, c5) for i in range(4)]
            yb = [sb(f"yb{i}", [128, D], F32, c5) for i in range(2)]
            sq5 = yb
            epart = sb("epart", [128, 16], F32, c5)
            sm_m = sb("sm_m", [128, 4], F32, c5)
            sm_v = sb("sm_v", [128, 4], F32, c5)
            sm_y = sb("sm_y", [128, 4], F32, c5)
            sm_z = sb("sm_z", [128, 4], F32, c5)
            sm_i = sb("sm_i", [128, 4], I32, c5)
            sm_n = sb("sm_n", [128, 4], F32, c5)
            p_f1 = [pst(f"p_f1{i}", [128, 512], F32, c5) for i in range(3)]
            p_f2 = [pst(f"p_f2{i}", [128, 512], F32, c5) for i in range(4)]

            w0v = w0b_scr.rearrange("(k p) n -> p k n", p=128)
            w1v = w1b_scr.rearrange("(c p) n -> p c n", p=128)
            x1v = x1_scr.rearrange("(c p) d -> c p d", p=128)
            yv = y_out.rearrange("(c p) d -> c p d", p=128)
            nf1 = [0]
            nw0 = [0]
            nw1 = [0]

            for t in range(seq // 512):
                xti = t % 2
                rec.wait("sync", s_ge, ck.get(("xt3", xti), 0))
                vxt = rec.dma("sync", s_xtl[xti], xt3b[xti][:].rearrange("p (k t) -> p k t", k=KC),
                              xl2_scr.rearrange("k p n -> p k n")[:, :, t * 512:(t + 1) * 512])
                vx1_t = {}
                for tc in range(2):
                    tcg = t * 4 + tc
                    xbi = tcg % 2
                    rec.wait("gpsimd", s_y, ck.get(("x1c", xbi), 0))
                    vx1_t[tc] = (xbi, rec.dma("gpsimd", s_x1l[xbi], x1c[xbi][:], x1v[tcg]))
                xt33 = xt3b[xti][:].rearrange("p (k t) -> p k t", k=KC)
                for blk in range(FF // 256):
                    wbi = nw0[0] % 2
                    nw0[0] += 1
                    rec.wait("sync", s_ge, ck.get(("w0bu", wbi), 0))
                    vw0 = rec.dma("sync", s_w0l[wbi],
                                  w0bu[wbi][:].rearrange("p (k n) -> p k n", k=KC),
                                  w0v[:, :, blk * 256:(blk + 1) * 256])
                    rec.wait("tensor", s_w0l[wbi], vw0)
                    if blk == 0:
                        rec.wait("tensor", s_xtl[xti], vxt)
                    w03 = w0bu[wbi][:].rearrange("p (k n) -> p k n", k=KC)
                    for dnc in range(2):
                        ncg = blk * 2 + dnc
                        fi = nf1[0]
                        nf1[0] += 1
                        pb = p_f1[fi % 3]
                        rec.wait("tensor", s_ge, ck.get(("pf1", fi % 3), 0))
                        for kcc in range(KC):
                            l_ap = w03[:, kcc, dnc * 128:(dnc + 1) * 128]
                            r_ap = xt33[:, kcc, :]
                            stf = kcc == 0
                            if kcc == KC - 1:
                                rec("tensor", lambda e, o=pb, l=l_ap, r=r_ap, s=stf: e.matmul(
                                    o[:], l, r, start=s, stop=True).then_inc(s_f1.h, 1))
                            else:
                                rec("tensor", lambda e, o=pb, l=l_ap, r=r_ap, s=stf: e.matmul(
                                    o[:], l, r, start=s, stop=False))
                        vf1 = s_f1.bump()
                        rec.wait("scalar", s_f1, vf1)
                        if ncg == 0 and t >= 1:
                            rec.wait("scalar", s_ev2, ck["dgdone"])
                        rec.op("scalar", lambda e, p=pb, n=ncg: e.activation(
                            dgb[:, n * 512:(n + 1) * 512], p[:], AF.Gelu,
                            bias=b0_sb[:, n:n + 1]), s_ge)
                        ck[("pf1", fi % 3)] = s_ge.bump()
                    ck[("w0bu", wbi)] = s_ge.v
                ck[("xt3", xti)] = s_ge.v
                vge_tile = s_ge.v
                if debug and t == 0:
                    rec.wait("gpsimd", s_ge, vge_tile)
                    rec.dma("gpsimd", s_cv[0], dbg_dg,
                            dgb[:].rearrange("p (c n) -> p c n", c=FNC))

                for db in range(4):
                    for sub in range(8):
                        wbi = nw1[0] % 2
                        nw1[0] += 1
                        g1 = ck.get(("w1bu", wbi), None)
                        if g1 is not None:
                            gsem, gval = g1
                            rec.wait("sync", s_f2s if gsem == "f2s" else s_ev2, gval)
                        vw1 = rec.dma("sync", s_w1l[wbi],
                                      w1bu[wbi][:].rearrange("p (c n) -> p c n", c=8),
                                      w1v[:, sub * 8:(sub + 1) * 8, db * 512:(db + 1) * 512])
                        rec.wait("tensor", s_w1l[wbi], vw1)
                        if db == 0 and sub == 0:
                            rec.wait("tensor", s_ge, vge_tile)
                        w13 = w1bu[wbi][:].rearrange("p (c n) -> p c n", c=8)
                        for tc in range(4):
                            pb = p_f2[tc]
                            if sub == 0:
                                rec.wait("tensor", s_ev2, ck.get(("pf2", tc), 0))
                            for c8 in range(8):
                                ncg = sub * 8 + c8
                                l_ap = dgb[:, ncg * 512 + tc * 128: ncg * 512 + (tc + 1) * 128]
                                r_ap = w13[:, c8, :]
                                stf = (sub == 0) and (c8 == 0)
                                spf = (sub == 7) and (c8 == 7)
                                lastmm = (tc == 3) and (c8 == 7) and (sub < 7)
                                if spf:
                                    rec("tensor", lambda e, o=pb, l=l_ap, r=r_ap, s=stf: e.matmul(
                                        o[:], l, r, start=s, stop=True).then_inc(s_f2.h, 1))
                                elif lastmm:
                                    rec("tensor", lambda e, o=pb, l=l_ap, r=r_ap, s=stf: e.matmul(
                                        o[:], l, r, start=s, stop=False).then_inc(s_f2s.h, 1))
                                else:
                                    rec("tensor", lambda e, o=pb, l=l_ap, r=r_ap, s=stf: e.matmul(
                                        o[:], l, r, start=s, stop=False))
                            if sub == 7:
                                s_f2.bump()
                        if sub < 7:
                            ck[("w1bu", wbi)] = ("f2s", s_f2s.bump())
                        else:
                            ck[("w1bu", wbi)] = ("pend", wbi)
                    for tc in range(4):
                        vstop = s_f2.v
                        rec.wait("vector", s_f2, vstop)
                        if db == 0:
                            rec.wait("vector", s_sq, ck.get(("esb", tc), 0))
                        rec.op("vector", lambda e, p=p_f2[tc], tc=tc, db=db: e.scalar_tensor_tensor(
                            esb[tc][:, db * 512:(db + 1) * 512], p[:], 1.0,
                            b1b[:, db * 512:(db + 1) * 512], ALU.mult, ALU.add,
                            accum_out=epart[:, 4 * tc + db:4 * tc + db + 1]), s_ev2)
                        ck[("pf2", tc)] = s_ev2.bump()
                    for key in list(ck):
                        if isinstance(ck.get(key), tuple) and ck[key][0] == "pend":
                            ck[key] = ("ev2", s_ev2.v)
                ck["dgdone"] = s_ev2.v

                for tc in range(4):
                    tcg = t * 4 + tc
                    ybi0 = tcg % 2
                    rec.wait("scalar", s_ev2, ck[("pf2", tc)])
                    if debug:
                        rec.dma("scalar", s_yst[tcg % 2], dbg_e[tcg * 128:(tcg + 1) * 128, :], esb[tc][:])
                    rec.wait("scalar", s_y, ck.get(("sq5", tc), 0))
                    rec.wait("scalar", s_yst[ybi0], ck.get(("yb", ybi0), 0))
                    rec.op("scalar", lambda e, tc=tc, b=ybi0: e.activation(
                        sq5[b][:], esb[tc][:], AF.Square, accum_out=sm_v[:, tc:tc + 1]), s_sq)
                    vsq5 = s_sq.bump()
                    ck[("esb", tc)] = vsq5
                ep3 = epart[:].rearrange("p (t d) -> p t d", t=4)
                rec.wait("vector", s_sq, s_sq.v)
                rec.op("vector", lambda e: e.tensor_add(sm_z[:], ep3[:, :, 0], ep3[:, :, 1]))
                rec.op("vector", lambda e: e.tensor_add(sm_m[:], ep3[:, :, 2], ep3[:, :, 3]))
                rec.op("vector", lambda e: e.tensor_add(sm_m[:], sm_z[:], sm_m[:]))
                rec.op("vector", lambda e: e.tensor_single_scalar(sm_m[:], sm_m[:], 1.0 / D, ALU.mult))
                rec.op("vector", lambda e: e.tensor_single_scalar(sm_v[:], sm_v[:], 1.0 / D, ALU.mult))
                rec.op("vector", lambda e: e.tensor_mul(sm_z[:], sm_m[:], sm_m[:]))
                rec.op("vector", lambda e: e.tensor_sub(sm_v[:], sm_v[:], sm_z[:]))
                rec.op("vector", lambda e: e.tensor_single_scalar(sm_v[:], sm_v[:], LN_EPS, ALU.add))
                rec.op("vector", lambda e: e.tensor_single_scalar(
                    sm_i[:], sm_v[:].bitcast(I32), 1, ALU.arith_shift_right))
                rec.op("vector", lambda e: e.tensor_scalar(sm_i[:], sm_i[:], -1, MAGIC, ALU.mult, ALU.add))
                rec.op("vector", lambda e: e.tensor_copy(sm_y[:], sm_i[:].bitcast(F32)))
                for _ in range(3):
                    rec.op("vector", lambda e: e.tensor_mul(sm_z[:], sm_y[:], sm_y[:]))
                    rec.op("vector", lambda e: e.tensor_mul(sm_z[:], sm_z[:], sm_v[:]))
                    rec.op("vector", lambda e: e.tensor_scalar(sm_z[:], sm_z[:], -0.5, 1.5, ALU.mult, ALU.add))
                    rec.op("vector", lambda e: e.tensor_mul(sm_y[:], sm_y[:], sm_z[:]))
                rec.op("vector", lambda e: e.scalar_tensor_tensor(
                    sm_n[:], sm_m[:], -1.0, sm_y[:], ALU.mult, ALU.mult))
                for tc in range(4):
                    tcg = t * 4 + tc
                    xbi, vx1 = vx1_t[tc]
                    t0 = slice(tc, tc + 1)
                    ybi = tcg % 2
                    rec.wait("vector", s_x1l[xbi], vx1)
                    rec.wait("vector", s_yst[ybi], ck.get(("yb", ybi), 0))
                    rec.op("vector", lambda e, tc=tc, o=t0, b=ybi: e.tensor_scalar(
                        yb[b][:], esb[tc][:], sm_y[:, o], sm_n[:, o], ALU.mult, ALU.add))
                    rec.op("vector", lambda e, b=ybi, bx=xbi: e.tensor_add(
                        yb[b][:], yb[b][:], x1c[bx][:]), s_y)
                    vy = s_y.bump()
                    ck[("x1c", xbi)] = vy
                    ck[("sq5", tc)] = vy
                    rec.wait("scalar", s_y, vy)
                    ck[("yb", ybi)] = rec.dma("scalar", s_yst[ybi], yv[tcg], yb[ybi][:])
                    if tc in (0, 1):
                        tcgn = t * 4 + tc + 2
                        xbin = tcgn % 2
                        rec.wait("gpsimd", s_y, ck.get(("x1c", xbin), 0))
                        vx1_t[tc + 2] = (xbin, rec.dma("gpsimd", s_x1l[xbin], x1c[xbin][:], x1v[tcgn]))

            rec.wait("scalar", s_yst[0], s_yst[0].v)
            rec.wait("scalar", s_yst[1], s_yst[1].v)
            rec.wait("sync", s_xtl[0], s_xtl[0].v)
            rec.wait("sync", s_xtl[1], s_xtl[1].v)
            rec.wait("sync", s_w0l[0], s_w0l[0].v)
            rec.wait("sync", s_w0l[1], s_w0l[1].v)
            rec.wait("sync", s_w1l[0], s_w1l[0].v)
            rec.wait("sync", s_w1l[1], s_w1l[1].v)
            rec.wait("gpsimd", s_x1l[0], s_x1l[0].v)
            rec.wait("gpsimd", s_x1l[1], s_x1l[1].v)
            rec.flush()

    return nc


_NC_CACHE = {}


def _get_program(seq):
    if seq not in _NC_CACHE:
        _NC_CACHE[seq] = build_program(seq)
    return _NC_CACHE[seq]


def kernel(L0_attn_x, L0_attn_Wq, L0_attn_Wk, L0_attn_Wv, L0_attn_Wo,
           L0_ffn_W0, L0_ffn_bias0, L0_ffn_W1, L0_ffn_bias1):
    x = np.ascontiguousarray(np.asarray(L0_attn_x, dtype=np.float32))
    b, seq, d = x.shape
    assert d == D
    tr = lambda w: np.ascontiguousarray(np.asarray(w, dtype=np.float32).T)
    weights = {
        "wqT": tr(L0_attn_Wq), "wkT": tr(L0_attn_Wk), "wvT": tr(L0_attn_Wv),
        "woT": tr(L0_attn_Wo), "w0T": tr(L0_ffn_W0), "w1T": tr(L0_ffn_W1),
        "b0": np.ascontiguousarray(np.asarray(L0_ffn_bias0, dtype=np.float32)),
        "b1": np.ascontiguousarray(np.asarray(L0_ffn_bias1, dtype=np.float32)),
    }
    nc = _get_program(seq)
    in_maps = [dict(weights, x=np.ascontiguousarray(x[i])) for i in range(b)]
    res = run_bass_kernel_spmd(nc, in_maps, core_ids=list(range(b)))
    out = np.stack([np.asarray(res.results[i]["y"]) for i in range(b)], axis=0)
    return out.astype(np.float32)


# version 09
# speedup vs baseline: 1.0836x; 1.0776x over previous
import math
from contextlib import ExitStack

import numpy as np

import concourse.bass as bass
import concourse.mybir as mybir
from concourse.bass_utils import run_bass_kernel_spmd

F32 = mybir.dt.float32
F16 = mybir.dt.float16
BF16 = mybir.dt.bfloat16
I32 = mybir.dt.int32
AF = mybir.ActivationFunctionType
ALU = mybir.AluOpType
AX = mybir.AxisListType

D = 2048
KC = 16
H = 16
FF = 8192
FNC = 64
LN_EPS = 1e-5
ATTN_EPS = 1e-6
LOG1E4 = math.log(10000.0)
TWO_PI = 2.0 * math.pi
MAGIC = 0x5F3759DF


class Sem:
    def __init__(self, nc, ctx, name):
        self.h = ctx.enter_context(nc.semaphore(name))
        self.v = 0

    def bump(self, k=1):
        self.v += k
        return self.v


class Rec:
    ENGINES = ("sync", "gpsimd", "tensor", "scalar", "vector")

    def __init__(self, nc, chain=None, last=None):
        self.nc = nc
        self.s = {k: [] for k in self.ENGINES}
        self.chain = chain or {}
        self.last = last if last is not None else {}

    def __call__(self, eng, fn):
        self.s[eng].append(fn)

    def op(self, eng, fn, sem=None):
        prev = self.last.get(eng)
        if prev is not None and prev[1] > 0:
            self.wait(eng, prev[0], prev[1])
        if sem is None:
            s = self.chain[eng]
            v = s.bump()
        else:
            s = sem
            v = s.v + 1
        self.s[eng].append(lambda e, f=fn, h=s.h: f(e).then_inc(h, 1))
        self.last[eng] = (s, v)
        return v

    def wait(self, eng, sem, val):
        if val > 0:
            self.s[eng].append(lambda e, s=sem.h, v=val: e.wait_ge(s, v))

    def dma(self, q, sem, out_ap, in_ap):
        v = sem.bump(16)
        self.s[q].append(lambda e, o=out_ap, i=in_ap, s=sem.h: e.dma_start(o, i).then_inc(s, 16))
        return v

    def flush(self):
        with self.nc.Block() as blk:
            for name in self.ENGINES:
                fns = self.s[name]
                if fns:
                    getattr(blk, name)(lambda e, fns=tuple(fns): [f(e) for f in fns])
        self.s = {k: [] for k in self.ENGINES}


def build_program(seq, debug=False):
    assert seq % 1024 == 0
    NCH = seq // 128
    HALF = seq // 2
    HCH = HALF // 128
    HT = HALF // 512
    NQT = seq // 512
    SB = KC // 4

    nc = bass.Bass("TRN2", target_bir_lowering=False)

    x_in = nc.dram_tensor("x", [seq, D], F32, kind="ExternalInput").ap()
    wq = nc.dram_tensor("wqT", [D, D], F32, kind="ExternalInput").ap()
    wk = nc.dram_tensor("wkT", [D, D], F32, kind="ExternalInput").ap()
    wv = nc.dram_tensor("wvT", [D, D], F32, kind="ExternalInput").ap()
    wo = nc.dram_tensor("woT", [D, D], F32, kind="ExternalInput").ap()
    w0 = nc.dram_tensor("w0T", [D, FF], F32, kind="ExternalInput").ap()
    w1 = nc.dram_tensor("w1T", [FF, D], F32, kind="ExternalInput").ap()
    b0_in = nc.dram_tensor("b0", [FF], F32, kind="ExternalInput").ap()
    b1_in = nc.dram_tensor("b1", [D], F32, kind="ExternalInput").ap()
    y_out = nc.dram_tensor("y", [seq, D], F32, kind="ExternalOutput").ap()

    knd = "ExternalOutput" if debug else "Internal"
    if debug:
        dbg_sin = nc.dram_tensor("dbg_sin", [64, seq], F32, kind="ExternalOutput").ap()
        dbg_cos = nc.dram_tensor("dbg_cos", [64, seq], F32, kind="ExternalOutput").ap()
        dbg_wo = nc.dram_tensor("dbg_wo", [seq, D], F32, kind="ExternalOutput").ap()
        dbg_e = nc.dram_tensor("dbg_e", [seq, D], F32, kind="ExternalOutput").ap()
        dbg_dg = nc.dram_tensor("dbg_dg", [128, FNC, 512], BF16, kind="ExternalOutput").ap()
    q_scr = nc.dram_tensor("q_scr", [H, 128, seq], BF16, kind=knd).ap()
    k_scr = nc.dram_tensor("k_scr", [H, 128, seq], BF16, kind=knd).ap()
    v_scr = nc.dram_tensor("v_scr", [H, 128, seq], BF16, kind=knd).ap()
    attn_scr = nc.dram_tensor("attn_scr", [KC, 128, seq], BF16, kind=knd).ap()
    x1_scr = nc.dram_tensor("x1_scr", [seq, D], F32, kind=knd).ap()
    xl2_scr = nc.dram_tensor("xl2_scr", [KC, 128, seq], BF16, kind=knd).ap()
    xlt_scr = nc.dram_tensor("xlt_scr", [seq, D], BF16).ap()
    w0b_scr = nc.dram_tensor("w0b", [D, FF], BF16, kind=knd).ap()
    w1b_scr = nc.dram_tensor("w1b", [FF, D], BF16, kind=knd).ap()

    xv = x_in.rearrange("(c p) d -> c p d", p=128)
    ck = {}

    with ExitStack() as octx:
        def sb(name, shape, dt=F32, ctx=None):
            return (ctx or octx).enter_context(nc.sbuf_tensor(name, shape, dt))

        def pst(name, shape, dt=F32, ctx=None):
            return (ctx or octx).enter_context(nc.psum_tensor(name, shape, dt))

        def S(name, ctx=None):
            return Sem(nc, ctx or octx, name)

        ident = sb("ident", [128, 128], BF16)
        ident64 = sb("ident64", [64, 64], F32)
        ones128 = sb("ones128", [128, 128], BF16)
        ones1f = sb("ones1f", [1, 128], F32)

        b0_sb = sb("b0_sb", [128, FNC], F32)
        b1b = sb("b1b", [128, D], F32)

        s_ld = S("s_ld")
        s_dveS = S("s_dveS")
        s_actS = S("s_actS")
        s_peS = S("s_peS")
        s_xld = [S(f"s_xld{i}") for i in range(2)]
        s_wld = [S(f"s_wld{i}") for i in range(2)]
        s_wdone = S("s_wdone")
        s_mm = S("s_mm")
        s_ev = S("s_ev")
        s_qe = S("s_qe")
        s_qkst = [S(f"s_qkst{i}") for i in range(2)]
        s_sq = S("s_sq")
        s_var = S("s_var")
        s_rstd = S("s_rstd")
        s_xln = S("s_xln")
        s_tp = S("s_tp")
        s_fm = S("s_fm")
        s_cv = [S(f"s_cv{i}") for i in range(2)]
        s_cvo = [S(f"s_cvo{i}") for i in range(2)]
        s_kl = [S(f"s_kl{i}") for i in range(2)]
        s_ql = [S(f"s_ql{i}") for i in range(2)]
        s_vl = [S(f"s_vl{i}") for i in range(2)]
        s_st2 = S("s_st2")
        s_sq2 = S("s_sq2")
        s_s4 = S("s_s4")
        s_av = S("s_av")
        s_dm = S("s_dm")
        s_rc = S("s_rc")
        s_at = S("s_at")
        s_ast = [S(f"s_ast{i}") for i in range(2)]
        s_vt = S("s_vt")
        s_vte = S("s_vte")
        s_hd = S("s_hd")
        s_ain = [S(f"s_ain{i}") for i in range(2)]
        s_wo3 = S("s_wo3")
        s_ev3 = S("s_ev3")
        s_x1 = S("s_x1")
        s_x1st = [S(f"s_x1st{i}") for i in range(2)]
        s_xl2 = S("s_xl2")
        s_x2st = [S(f"s_x2st{i}") for i in range(2)]
        s_xtl = [S(f"s_xtl{i}") for i in range(2)]
        s_x1l = [S(f"s_x1l{i}") for i in range(2)]
        s_w0l = [S(f"s_w0l{i}") for i in range(2)]
        s_w1l = [S(f"s_w1l{i}") for i in range(2)]
        s_f1 = S("s_f1")
        s_ge = S("s_ge")
        s_f2s = S("s_f2s")
        s_f2 = S("s_f2")
        s_ev2 = S("s_ev2")
        s_y = S("s_y")
        s_yst = [S(f"s_yst{i}") for i in range(2)]
        s_wol = S("s_wol")
        s_gps = S("s_gps")
        s_xlt = [S(f"s_xlt{i}") for i in range(2)]
        s_xltl = [S(f"s_xltl{i}") for i in range(2)]
        chain = {"vector": S("s_chv"), "scalar": S("s_chs"), "gpsimd": S("s_chg")}
        last_op = {}

        c12 = octx.enter_context(ExitStack())
        sinf = sb("sinf", [64, seq], F16, c12)
        cosf = sb("cosf", [64, seq], F16, c12)
        cstage = [sb(f"cstage{i}", [128, D], BF16, c12) for i in range(2)]
        masks = sb("masks", [128, 4 * 512], BF16, c12)

        with ExitStack() as c1:
            rec = Rec(nc, chain, last_op)
            b0t = sb("b0t", [64, 128], F32, c1)
            b1row = sb("b1row", [1, D], F32, c1)
            th = sb("th", [64, 1], F32, c1)
            iop = sb("iop", [64, 1], F32, c1)
            posr = sb("posr", [1, seq], F32, c1)
            ubuf = sb("ubuf", [64, seq], F32, c1)
            ibuf = sb("ibuf", [64, seq], I32, c1)
            fbuf = sb("fbuf", [64, seq], F32, c1)
            rsin = sb("rsin", [64, seq], F32, c1)
            rcos = sb("rcos", [64, seq], F32, c1)
            p_s1 = pst("p_s1", [64, 512], F32, c1)
            p_s2 = pst("p_s2", [128, 512], F32, c1)
            p_bt = pst("p_bt", [128, 64], F32, c1)

            g = "gpsimd"
            rec.op(g, lambda e: e.memset(ident[:], 0.0))
            rec.op(g, lambda e: e.affine_select(out=ident[:], in_=ident[:],
                                             compare_op=ALU.not_equal, fill=1.0, base=0,
                                             pattern=[[-1, 128]], channel_multiplier=1))
            rec.op(g, lambda e: e.memset(ident64[:], 0.0))
            rec.op(g, lambda e: e.affine_select(out=ident64[:], in_=ident64[:],
                                             compare_op=ALU.not_equal, fill=1.0, base=0,
                                             pattern=[[-1, 64]], channel_multiplier=1))
            rec.op(g, lambda e: e.memset(ones128[:], 1.0))
            rec.op(g, lambda e: e.memset(ones1f[:], 1.0))
            rec.op(g, lambda e: e.memset(masks[:], 1.0))
            for mi in range(4):
                rec.op(g, lambda e, mi=mi: e.affine_select(
                    out=masks[:, mi * 512:(mi + 1) * 512],
                    in_=masks[:, mi * 512:(mi + 1) * 512],
                    compare_op=ALU.is_ge, fill=0.0, base=-128 * mi,
                    pattern=[[1, 512]], channel_multiplier=-1))
            rec.op(g, lambda e: e.iota(iop[:], [[0, 1]], base=0, channel_multiplier=1,
                                       allow_small_or_imprecise_dtypes=True))
            rec.op(g, lambda e: e.iota(posr[:], [[1, seq]], base=0, channel_multiplier=0,
                                    allow_small_or_imprecise_dtypes=True), s_gps)
            v_gps = s_gps.bump()
            rec.dma(g, s_ld, b0t[:], b0_in.rearrange("(p c) -> p c", p=64))
            vld = rec.dma(g, s_ld, b1row[:], b1_in[None, :])

            rec.wait("vector", s_gps, v_gps)
            rec.wait("tensor", s_gps, v_gps)
            rec.wait("scalar", s_gps, v_gps)
            rec.op("scalar", lambda e: e.activation(th[:], iop[:], AF.Exp,
                                                    scale=-LOG1E4 / 64.0), s_actS)
            v_th = s_actS.bump()

            rec.wait("tensor", s_ld, vld)
            rec("tensor", lambda e: e.transpose(p_bt[:], b0t[:], ident64[:]).then_inc(s_peS.h, 1))
            v_bt = s_peS.bump()
            rec.wait("vector", s_peS, v_bt)
            rec.op("vector", lambda e: e.tensor_copy(b0_sb[:, 0:64], p_bt[:]), s_dveS)
            v_b0c = s_dveS.bump()

            rec.wait("tensor", s_dveS, v_b0c)
            nslab = seq // 512
            for sl in range(nslab + 4):
                if sl < nslab:
                    src = posr[:, sl * 512:(sl + 1) * 512]
                else:
                    src = b1row[:, (sl - nslab) * 512:(sl - nslab + 1) * 512]
                if sl < nslab:
                    rec("tensor", lambda e, s=src: e.matmul(
                        p_s1[:], ones1f[:, 0:64], s, start=True, stop=True).then_inc(s_peS.h, 1))
                else:
                    rec("tensor", lambda e, s=src: e.matmul(
                        p_s2[:], ones1f[:], s, start=True, stop=True).then_inc(s_peS.h, 1))
                vmm = s_peS.bump()
                rec.wait("vector", s_peS, vmm)
                if sl == 0:
                    rec.wait("vector", s_actS, v_th)
                if sl < nslab:
                    sls = slice(sl * 512, (sl + 1) * 512)
                    rec.op("vector", lambda e, s=sls: e.tensor_scalar(
                        ubuf[:, s], p_s1[:], th[:], 1.0 / TWO_PI, ALU.mult, ALU.mult))
                    rec.op("vector", lambda e, s=sls: e.tensor_copy(ibuf[:, s], ubuf[:, s]))
                    rec.op("vector", lambda e, s=sls: e.tensor_copy(fbuf[:, s], ibuf[:, s]))
                    rec.op("vector", lambda e, s=sls: e.tensor_sub(fbuf[:, s], ubuf[:, s], fbuf[:, s]))
                    rec.op("vector", lambda e, s=sls: e.tensor_single_scalar(
                        rsin[:, s], fbuf[:, s], TWO_PI, ALU.mult))
                    rec.op("vector", lambda e, s=sls: e.tensor_single_scalar(
                        ubuf[:, s], ubuf[:, s], 0.25, ALU.add))
                    rec.op("vector", lambda e, s=sls: e.tensor_copy(ibuf[:, s], ubuf[:, s]))
                    rec.op("vector", lambda e, s=sls: e.tensor_copy(fbuf[:, s], ibuf[:, s]))
                    rec.op("vector", lambda e, s=sls: e.tensor_sub(fbuf[:, s], ubuf[:, s], fbuf[:, s]))
                    rec.op("vector", lambda e, s=sls: e.tensor_single_scalar(
                        rcos[:, s], fbuf[:, s], TWO_PI, ALU.mult), s_dveS)
                else:
                    c0 = (sl - nslab) * 512
                    rec.op("vector", lambda e, c0=c0: e.tensor_copy(
                        b1b[:, c0:c0 + 512], p_s2[:]), s_dveS)
                vdv = s_dveS.bump()
                rec.wait("tensor", s_dveS, vdv)
            rec.wait("scalar", s_dveS, s_dveS.v)
            rec.op("scalar", lambda e: e.activation(sinf[:], rsin[:], AF.Sin))
            rec.op("scalar", lambda e: e.activation(cosf[:], rcos[:], AF.Sin), s_actS)
            v_tbl = s_actS.bump()
            if debug:
                rec.wait("gpsimd", s_actS, v_tbl)
                rec.dma("gpsimd", s_ld, dbg_sin, sinf[:])
                rec.dma("gpsimd", s_ld, dbg_cos, cosf[:])
                rec.wait("gpsimd", s_ld, s_ld.v)
            rec.flush()

        if True:
            conv_steps = ([("w0", i, hh) for i in range(KC) for hh in range(4)]
                          + [("w1", i, None) for i in range(FNC)])
            conv_pos = [0]

            def emit_conv(rec, n):
                for _ in range(n):
                    i = conv_pos[0]
                    if i >= len(conv_steps):
                        return
                    conv_pos[0] += 1
                    kind, c, hh = conv_steps[i]
                    sl = i % 2
                    buf = cstage[sl]
                    rec.wait("gpsimd", s_cvo[sl], ck.get(("cvo", sl), 0))
                    if kind == "w0":
                        hs = slice(hh * (FF // 4), (hh + 1) * (FF // 4))
                        vin = rec.dma("gpsimd", s_cv[sl], buf[:], w0[c * 128:(c + 1) * 128, hs])
                        rec.wait("gpsimd", s_cv[sl], vin)
                        vout = rec.dma("gpsimd", s_cvo[sl], w0b_scr[c * 128:(c + 1) * 128, hs], buf[:])
                    else:
                        vin = rec.dma("gpsimd", s_cv[sl], buf[:, 0:D], w1[c * 128:(c + 1) * 128, :])
                        rec.wait("gpsimd", s_cv[sl], vin)
                        vout = rec.dma("gpsimd", s_cvo[sl], w1b_scr[c * 128:(c + 1) * 128, :], buf[:, 0:D])
                    ck[("cvo", sl)] = vout

            with ExitStack() as c2:
                rec = Rec(nc, chain, last_op)
                xbuf = [sb(f"xbuf{i}", [128, D], F32, c2) for i in range(2)]
                sumsq = sb("p1ss", [128, 1], F32, c2)
                xsum = sb("p1s", [128, 1], F32, c2)
                mb = sb("p1m", [128, 1], F32, c2)
                vb = sb("p1v", [128, 1], F32, c2)
                lb = sb("p1l", [128, 1], F32, c2)
                rs = sb("p1r", [128, 1], F32, c2)
                nmr = sb("p1n", [128, 1], F32, c2)
                xlnb = [sb(f"xlnb{i}", [128, D], BF16, c2) for i in range(2)]
                xfm = sb("xfm", [128, KC * HALF], BF16, c2)
                wsb = [sb(f"wsb{i}", [128, KC * 512], BF16, c2) for i in range(2)]
                rt1 = sb("rt1", [64, 512], F16, c2)
                rt2 = sb("rt2", [64, 512], F16, c2)
                qlo16 = [sb(f"qlo16{i}", [64, 512], F16, c2) for i in range(2)]
                qhi16 = [sb(f"qhi16{i}", [64, 512], F16, c2) for i in range(2)]
                roblo = [sb(f"roblo{i}", [64, HALF], BF16, c2) for i in range(2)]
                robhi = [sb(f"robhi{i}", [64, HALF], BF16, c2) for i in range(2)]
                p_pr = [pst(f"p_pr{i}", [128, 512], F32, c2) for i in range(4)]
                p_tp = [pst(f"p_tp{i}", [128, 1024], BF16, c2) for i in range(4)]

                wT = {"q": wq, "k": wk, "v": wv}
                scr = {"q": q_scr, "k": k_scr, "v": v_scr}
                xfm3 = xfm[:].rearrange("p (k t) -> p k t", k=KC)
                gi_ctr = [0]
                tp_ctr = [0]

                def wait_ck(eng, key):
                    g = ck.get(key)
                    if g is not None:
                        rec.wait(eng, g[0], g[1])

                def emit_tp_chunk(c, src, gate_sem, gate_val):
                    rec.wait("tensor", gate_sem, gate_val)
                    for grp in range(2):
                        ti = tp_ctr[0]
                        tp_ctr[0] += 1
                        pt = p_tp[ti % 4]
                        rec.wait("tensor", s_fm, ck.get(("ptp", ti % 4), 0))
                        for j in range(8):
                            kcc = grp * 8 + j
                            if j < 7:
                                rec("tensor", lambda e, x=src, k=kcc, p=pt, j=j: e.transpose(
                                    p[:, j * 128:(j + 1) * 128],
                                    x[:, k * 128:(k + 1) * 128], ident[:]))
                            else:
                                rec("tensor", lambda e, x=src, k=kcc, p=pt, j=j: e.transpose(
                                    p[:, j * 128:(j + 1) * 128],
                                    x[:, k * 128:(k + 1) * 128],
                                    ident[:]).then_inc(s_tp.h, 1))
                        vtp = s_tp.bump()
                        rec.wait("scalar", s_tp, vtp)
                        dst = xfm3[:, grp * 8:(grp + 1) * 8, c * 128:(c + 1) * 128]
                        srcp = pt[:].rearrange("p (j t) -> p j t", j=8)
                        rec.op("scalar", lambda e, d=dst, s=srcp: e.activation(
                            d, s, AF.Identity), s_fm)
                        ck[("ptp", ti % 4)] = s_fm.bump()
                    return s_tp.v

                def emit_ln_chunk(gc, to_scratch):
                    c = gc % HCH
                    bi = gc % 2
                    rec.wait("sync", s_xln, ck.get(("xbuf", bi), 0))
                    vx = rec.dma("sync", s_xld[bi], xbuf[bi][:], xv[gc])
                    rec.wait("scalar", s_xld[bi], vx)
                    rec.wait("scalar", s_var, ck.get("p1var", 0))
                    wait_ck("scalar", ("xlnb", bi))
                    rec.op("scalar", lambda e, b=bi: e.activation(
                        xlnb[b][:], xbuf[b][:], AF.Square, accum_out=sumsq[:]), s_sq)
                    vsq = s_sq.bump()
                    rec.wait("vector", s_xld[bi], vx)
                    rec.wait("vector", s_rstd, ck.get("p1rstd", 0))
                    rec.op("vector", lambda e, b=bi: e.reduce_sum(xsum[:], xbuf[b][:], axis=AX.X))
                    rec.wait("vector", s_sq, vsq)
                    rec.op("vector", lambda e: e.tensor_single_scalar(mb[:], xsum[:], 1.0 / D, ALU.mult))
                    rec.op("vector", lambda e: e.tensor_single_scalar(vb[:], sumsq[:], 1.0 / D, ALU.mult))
                    rec.op("vector", lambda e: e.tensor_mul(xsum[:], mb[:], mb[:]))
                    rec.op("vector", lambda e: e.tensor_sub(vb[:], vb[:], xsum[:]))
                    rec.op("vector", lambda e: e.tensor_single_scalar(
                        vb[:], vb[:], LN_EPS, ALU.add), s_var)
                    ck["p1var"] = s_var.bump()
                    rec.wait("scalar", s_var, ck["p1var"])
                    rec.op("scalar", lambda e: e.activation(lb[:], vb[:], AF.Ln))
                    rec.op("scalar", lambda e: e.activation(
                        rs[:], lb[:], AF.Exp, scale=-0.5), s_rstd)
                    ck["p1rstd"] = s_rstd.bump()
                    rec.wait("vector", s_rstd, ck["p1rstd"])
                    wait_ck("vector", ("xlnb", bi))
                    rec.op("vector", lambda e: e.scalar_tensor_tensor(
                        nmr[:], mb[:], -1.0, rs[:], ALU.mult, ALU.mult))
                    rec.op("vector", lambda e, b=bi: e.tensor_scalar(
                        xlnb[b][:], xbuf[b][:], rs[:], nmr[:], ALU.mult, ALU.add), s_xln)
                    ck[("xbuf", bi)] = s_xln.bump()
                    if to_scratch:
                        rec.wait("gpsimd", s_xln, ck[("xbuf", bi)])
                        vst = rec.dma("gpsimd", s_xlt[bi], xlt_scr[gc * 128:(gc + 1) * 128, :], xlnb[bi][:])
                        ck[("xltst", gc)] = vst
                        ck[("xlnb", bi)] = (s_xlt[bi], vst)
                    else:
                        vtp = emit_tp_chunk(c, xlnb[bi][:], s_xln, ck[("xbuf", bi)])
                        ck[("xlnb", bi)] = (s_tp, vtp)

                def emit_lite_chunk(gc):
                    c = gc % HCH
                    bi = gc % 2
                    wait_ck("sync", ("xlnb", bi))
                    rec.wait("sync", s_xlt[bi], ck[("xltst", gc)])
                    vld = rec.dma("sync", s_xltl[bi], xlnb[bi][:], xlt_scr[gc * 128:(gc + 1) * 128, :])
                    vtp = emit_tp_chunk(c, xlnb[bi][:], s_xltl[bi], vld)
                    ck[("xlnb", bi)] = (s_tp, vtp)

                for hf in range(2):
                    if hf == 0:
                        for c in range(HCH):
                            emit_ln_chunk(c, False)
                    else:
                        for c in range(HCH):
                            emit_ln_chunk(HCH + c, False)
                    ck[("fmhalf", hf)] = s_fm.v

                    for sbi in range(3 * SB):
                        proj = "qkv"[sbi // SB]
                        blkc = sbi % SB
                        gsb = hf * 3 * SB + sbi
                        wbi = gsb % 2
                        rec.wait("gpsimd", s_qe, ck.get(("wsb", wbi), 0))
                        wsrc = wT[proj].rearrange("(k p) n -> p k n", p=128)[:, :, blkc * 512:(blkc + 1) * 512]
                        vw = rec.dma("gpsimd", s_wld[wbi],
                                     wsb[wbi][:].rearrange("p (k n) -> p k n", k=KC), wsrc)
                        rec.wait("tensor", s_wld[wbi], vw)
                        if sbi == 0:
                            rec.wait("tensor", s_fm, ck[("fmhalf", hf)])
                        wv3 = wsb[wbi][:].rearrange("p (k n) -> p k n", k=KC)
                        for dqc in range(4):
                            dqg = blkc * 4 + dqc
                            slot = dqg % 2
                            for t in range(HT):
                                gi = gi_ctr[0]
                                gi_ctr[0] += 1
                                pb = p_pr[gi % 4]
                                rec.wait("tensor", s_qe, ck.get(("ppr", gi % 4), 0))
                                for kcc in range(KC):
                                    l_ap = wv3[:, kcc, dqc * 128:(dqc + 1) * 128]
                                    r_ap = xfm3[:, kcc, t * 512:(t + 1) * 512]
                                    stf = kcc == 0
                                    if kcc == KC - 1:
                                        rec("tensor", lambda e, o=pb, l=l_ap, r=r_ap, s=stf: e.matmul(
                                            o[:], l, r, start=s, stop=True).then_inc(s_mm.h, 1))
                                    else:
                                        rec("tensor", lambda e, o=pb, l=l_ap, r=r_ap, s=stf: e.matmul(
                                            o[:], l, r, start=s, stop=False))
                                vmm = s_mm.bump()
                                cols = slice(t * 512, (t + 1) * 512)
                                gcols = slice(hf * HALF + t * 512, hf * HALF + (t + 1) * 512)
                                rlo, rhi = roblo[slot], robhi[slot]
                                g2 = gi % 2
                                rec.wait("scalar", s_mm, vmm)
                                if proj in ("q", "k"):
                                    rec.wait("scalar", s_ev, ck.get(("q16", g2), 0))
                                    rec.op("scalar", lambda e, p=pb, o=qlo16[g2]: e.activation(
                                        o[:], p[0:64, :], AF.Identity))
                                    rec.op("scalar", lambda e, p=pb, o=qhi16[g2]: e.activation(
                                        o[:], p[64:128, :], AF.Identity), s_qe)
                                    vqe = s_qe.bump()
                                    ck[("ppr", gi % 4)] = vqe
                                    rec.wait("vector", s_qe, vqe)
                                    if t == 0:
                                        rec.wait("vector", s_qkst[slot], ck.get(("rob", slot), 0))
                                    rec.op("vector", lambda e, q=qlo16[g2], g=gcols: e.tensor_mul(
                                        rt1[:], q[:], cosf[:, g]))
                                    rec.op("vector", lambda e, q=qhi16[g2], g=gcols: e.tensor_mul(
                                        rt2[:], q[:], sinf[:, g]))
                                    rec.op("vector", lambda e, r=rlo, c=cols: e.tensor_sub(
                                        r[:, c], rt1[:], rt2[:]))
                                    rec.op("vector", lambda e, q=qlo16[g2], g=gcols: e.tensor_mul(
                                        rt1[:], q[:], sinf[:, g]))
                                    rec.op("vector", lambda e, q=qhi16[g2], g=gcols: e.tensor_mul(
                                        rt2[:], q[:], cosf[:, g]))
                                    rec.op("vector", lambda e, r=rhi, c=cols: e.tensor_add(
                                        r[:, c], rt1[:], rt2[:]), s_ev)
                                    ck[("q16", g2)] = s_ev.bump()
                                    if t == HT - 1:
                                        rec.wait("scalar", s_ev, ck[("q16", g2)])
                                        hc = slice(hf * HALF, (hf + 1) * HALF)
                                        rec.dma("scalar", s_qkst[slot],
                                                scr[proj][dqg, 0:64, hc], rlo[:])
                                        ck[("rob", slot)] = rec.dma(
                                            "scalar", s_qkst[slot],
                                            scr[proj][dqg, 64:128, hc], rhi[:])
                                else:
                                    if t == 0:
                                        rec.wait("scalar", s_qkst[slot], ck.get(("rob", slot), 0))
                                    rec.op("scalar", lambda e, p=pb, r=rlo, c=cols: e.activation(
                                        r[:, c], p[0:64, :], AF.Identity))
                                    rec.op("scalar", lambda e, p=pb, r=rhi, c=cols: e.activation(
                                        r[:, c], p[64:128, :], AF.Identity), s_qe)
                                    ck[("ppr", gi % 4)] = s_qe.bump()
                                    if t == HT - 1:
                                        hc = slice(hf * HALF, (hf + 1) * HALF)
                                        rec.dma("scalar", s_qkst[slot],
                                                scr[proj][dqg, 0:64, hc], rlo[:])
                                        ck[("rob", slot)] = rec.dma(
                                            "scalar", s_qkst[slot],
                                            scr[proj][dqg, 64:128, hc], rhi[:])
                        ck[("wsb", wbi)] = s_qe.v

                    emit_conv(rec, 25)

                rec.wait("scalar", s_qkst[0], s_qkst[0].v)
                rec.wait("scalar", s_qkst[1], s_qkst[1].v)
                rec.wait("sync", s_xld[0], s_xld[0].v)
                rec.wait("sync", s_xld[1], s_xld[1].v)
                rec.wait("gpsimd", s_wld[0], s_wld[0].v)
                rec.wait("gpsimd", s_wld[1], s_wld[1].v)
                rec.flush()

            with ExitStack() as c3:
                rec = Rec(nc, chain, last_op)
                kb = [sb(f"kb{i}", [128, seq], BF16, c3) for i in range(2)]
                qb = [sb(f"qb{i}", [128, seq], BF16, c3) for i in range(2)]
                vbf = [sb(f"vb{i}", [128, seq], BF16, c3) for i in range(2)]
                vtm = [sb(f"vtm{i}", [128, seq], BF16, c3) for i in range(2)]
                s2b = [sb(f"s2b{i}", [128, 512], BF16, c3) for i in range(7)]
                s2m = sb("s2m", [128, 512], BF16, c3)
                s4b = [sb(f"s4b{i}", [128, 512], BF16, c3) for i in range(7)]
                dmb = [sb(f"dmb{i}", [128, 512], F32, c3) for i in range(2)]
                rcb = [sb(f"rcb{i}", [128, 512], F32, c3) for i in range(2)]
                atb = [sb(f"atb{i}", [128, 512], BF16, c3) for i in range(2)]
                p_st = [pst(f"p_st{i}", [128, 512], F32, c3) for i in range(4)]
                p_o = [pst(f"p_o{i}", [128, 512], F32, c3) for i in range(2)]
                p_d = [pst(f"p_d{i}", [128, 512], F32, c3) for i in range(1)]
                p_vt = pst("p_vt", [128, 1024], BF16, c3)

                emit_conv(rec, len(conv_steps))
                n2 = [0]
                nqt = [0]

                def emit_chunk(h, bi, qt, jc):
                    i = n2[0] + jc
                    pb = p_st[i % 4]
                    rec.wait("tensor", s_sq2, ck.get(("pst", i % 4), 0))
                    rec("tensor", lambda e, p=pb, b=bi, jc=jc, q0=qt * 512: e.matmul(
                        p[:], kb[b][:, jc * 128:(jc + 1) * 128], qb[b][:, q0:q0 + 512],
                        start=True, stop=True).then_inc(s_st2.h, 1))
                    vst = s_st2.bump()
                    rec.wait("scalar", s_st2, vst)
                    rec.wait("scalar", s_s4, ck.get(("s2b", i % 7), 0))
                    rec("scalar", lambda e, p=pb, o=s2b[i % 7]: e.activation(
                        o[:], p[:], AF.Square).then_inc(s_sq2.h, 1))
                    rec.last["scalar"] = (s_sq2, s_sq2.v + 1)
                    vs2 = s_sq2.bump()
                    ck[("pst", i % 4)] = vs2
                    rec.wait("vector", s_sq2, vs2)
                    rec.wait("vector", s_av, ck.get(("s4b", i % 7), 0))
                    dmi = jc - 4 * qt
                    if dmi >= 0:
                        rec.op("vector", lambda e, s=s2b[i % 7], m=dmi: e.tensor_mul(
                            s2m[:], s[:], masks[:, m * 512:(m + 1) * 512]))
                        rec.op("vector", lambda e, s=s2b[i % 7], o=s4b[i % 7]: e.tensor_mul(
                            o[:], s[:], s2m[:]), s_s4)
                        ck[("s2b", i % 7)] = s_s4.bump()
                    else:
                        rec("vector", lambda e, s=s2b[i % 7], o=s4b[i % 7]: e.tensor_mul(
                            o[:], s[:], s[:]).then_inc(s_s4.h, 1))
                        rec.last["vector"] = (s_s4, s_s4.v + 1)
                        ck[("s2b", i % 7)] = s_s4.bump()
                    return ck[("s2b", i % 7)], i

                NG_VT = seq // 1024

                def emit_vt_group(h, grp, first):
                    b = h % 2
                    if first:
                        rec.wait("tensor", s_vl[b], ck[("vload", h)])
                    rec.wait("tensor", s_vte, ck.get("pvt", 0))
                    for j in range(8):
                        cc = grp * 8 + j
                        if j < 7:
                            rec("tensor", lambda e, b=b, c=cc, j=j: e.transpose(
                                p_vt[:, j * 128:(j + 1) * 128],
                                vbf[b][:, c * 128:(c + 1) * 128], ident[:]))
                        else:
                            rec("tensor", lambda e, b=b, c=cc, j=j: e.transpose(
                                p_vt[:, j * 128:(j + 1) * 128],
                                vbf[b][:, c * 128:(c + 1) * 128], ident[:]).then_inc(s_vt.h, 1))
                    vvt = s_vt.bump()
                    rec.wait("scalar", s_vt, vvt)
                    dst = vtm[b][:].rearrange("p (g j t) -> p g j t", g=NG_VT, j=8)[:, grp]
                    srcp = p_vt[:].rearrange("p (j t) -> p j t", j=8)
                    rec.op("scalar", lambda e, d=dst, s=srcp: e.activation(
                        d, s, AF.Identity), s_vte)
                    ck["pvt"] = s_vte.bump()
                    ck[("vte_head", h)] = ck["pvt"]

                def rec_head_loads(hh):
                    b = hh % 2
                    rec.wait("sync", s_at, ck.get(("kqv", b), 0))
                    ck[("kload", hh)] = rec.dma("sync", s_kl[b], kb[b][:], k_scr[hh])
                    ck[("qload", hh)] = rec.dma("sync", s_ql[b], qb[b][:], q_scr[hh])
                    ck[("vload", hh)] = rec.dma("sync", s_vl[b], vbf[b][:], v_scr[hh])

                rec_head_loads(0)
                for h in range(H):
                    bi = h % 2
                    if h + 1 < H:
                        rec_head_loads(h + 1)
                    if ("vte_head", h) not in ck:
                        for grp in range(NG_VT):
                            emit_vt_group(h, grp, grp == 0)
                    vte_head = ck[("vte_head", h)]

                    rec.wait("tensor", s_kl[bi], ck[("kload", h)])
                    rec.wait("tensor", s_ql[bi], ck[("qload", h)])
                    for qt in range(NQT):
                        njc = 4 * qt + 4
                        poi = nqt[0] % 2
                        LAG = 6
                        vs4 = {}
                        for j in range(min(LAG, njc)):
                            vs4[j] = emit_chunk(h, bi, qt, j)
                        for jc in range(njc):
                            v4, i4 = vs4[jc]
                            rec.wait("tensor", s_s4, v4)
                            if jc == 0:
                                rec.wait("tensor", s_vte, vte_head)
                                rec.wait("tensor", s_at, ck.get(("po", poi), 0))
                                rec.wait("tensor", s_dm, ck.get("pd2", 0))
                            last = jc == njc - 1
                            rec("tensor", lambda e, po=p_o[poi], jc=jc, s4=s4b[i4 % 7],
                                vt=vtm[bi], st=(jc == 0), sp=last: e.matmul(
                                po[:], vt[:, jc * 128:(jc + 1) * 128], s4[:], start=st, stop=sp))
                            rec("tensor", lambda e, pd=p_d[0], s4=s4b[i4 % 7],
                                st=(jc == 0), sp=last: e.matmul(
                                pd[:], ones128[:], s4[:], start=st, stop=sp).then_inc(s_av.h, 1))
                            ck[("s4b", i4 % 5)] = s_av.bump()
                            if jc + LAG < njc:
                                vs4[jc + LAG] = emit_chunk(h, bi, qt, jc + LAG)
                        if h + 1 < H and 2 <= qt < 2 + NG_VT and NQT > 2 + NG_VT:
                            emit_vt_group(h + 1, qt - 2, qt == 2)
                        n2[0] += njc
                        vav = s_av.v
                        rec.wait("vector", s_av, vav)
                        rec.wait("vector", s_rc, ck.get(("dmb", poi), 0))
                        rec.op("vector", lambda e, p=p_d[0], o=dmb[poi]: e.tensor_single_scalar(
                            o[:], p[:], ATTN_EPS, ALU.max), s_dm)
                        vdm = s_dm.bump()
                        ck["pd2"] = vdm
                        rec.wait("scalar", s_dm, vdm)
                        rec.wait("scalar", s_at, ck.get(("rcb", poi), 0))
                        rec.op("scalar", lambda e, i=poi: e.activation(rcb[i][:], dmb[i][:], AF.Ln))
                        rec.op("scalar", lambda e, i=poi: e.activation(
                            rcb[i][:], rcb[i][:], AF.Exp, scale=-1.0), s_rc)
                        ck[("dmb", poi)] = s_rc.bump()
                        rec.wait("vector", s_rc, ck[("dmb", poi)])
                        rec.wait("vector", s_ast[poi], ck.get(("atb", poi), 0))
                        rec.op("vector", lambda e, i=poi: e.tensor_mul(
                            atb[i][:], p_o[i][:], rcb[i][:]), s_at)
                        vat = s_at.bump()
                        ck[("rcb", poi)] = vat
                        ck[("po", poi)] = vat
                        rec.wait("scalar", s_at, vat)
                        ck[("atb", poi)] = rec.dma("scalar", s_ast[poi],
                                                   attn_scr[h, :, qt * 512:(qt + 1) * 512], atb[poi][:])
                        nqt[0] += 1
                    ck[("kqv", bi)] = s_at.v

                rec.wait("scalar", s_ast[0], s_ast[0].v)
                rec.wait("scalar", s_ast[1], s_ast[1].v)
                rec.wait("gpsimd", s_cvo[0], s_cvo[0].v)
                rec.wait("gpsimd", s_cvo[1], s_cvo[1].v)
                rec.wait("sync", s_kl[0], s_kl[0].v)
                rec.wait("sync", s_kl[1], s_kl[1].v)
                rec.wait("sync", s_ql[0], s_ql[0].v)
                rec.wait("sync", s_ql[1], s_ql[1].v)
                rec.wait("sync", s_vl[0], s_vl[0].v)
                rec.wait("sync", s_vl[1], s_vl[1].v)
                rec.flush()

        c12.close()

        with ExitStack() as c4:
            rec = Rec(nc, chain, last_op)
            wot = sb("wot", [128, KC * D], BF16, c4)
            ain = [sb(f"ain{i}", [128, KC * 512], BF16, c4) for i in range(2)]
            xb3 = [sb(f"xb3{i}", [128, D], F32, c4) for i in range(2)]
            wsb3 = [sb(f"wsb3{i}", [128, D], F32, c4) for i in range(2)]
            qsum2 = sb("qsum2", [128, 8], F32, c4)
            sm3 = sb("sm3", [128, 12], F32, c4)
            lnb = sb("lnb", [128, D], F32, c4)
            sq3 = sb("sq3d", [128, D], F32, c4)
            x1b = [sb(f"x1b{i}", [128, D], F32, c4) for i in range(2)]
            xl2t = [sb(f"xl2t{i}", [128, D], BF16, c4) for i in range(2)]
            xl2f2 = [sb(f"xl2f2{i}", [128, KC * 128], BF16, c4) for i in range(2)]
            p_q3 = [pst(f"p_q3{i}", [128, 512], F32, c4) for i in range(6)]
            p_t3 = [pst(f"p_t3{i}", [128, 1024], BF16, c4) for i in range(2)]

            vwo = rec.dma("gpsimd", s_wol, wot[:].rearrange("p (k n) -> p k n", k=KC),
                          wo.rearrange("(k p) n -> p k n", p=128))
            wot3 = wot[:].rearrange("p (k n) -> p k n", k=KC)
            nq3 = [0]
            tp3 = [0]

            def emit_tp3(tce):
                b = tce % 2
                rec.wait("tensor", s_xl2, ck[("xl2done", tce)])
                for grp in range(2):
                    ti = tp3[0]
                    tp3[0] += 1
                    pt = p_t3[ti % 2]
                    rec.wait("tensor", s_fm, ck.get(("pt3", ti % 2), 0))
                    for j in range(8):
                        kcc = grp * 8 + j
                        if j < 7:
                            rec("tensor", lambda e, b=b, k=kcc, p=pt, j=j: e.transpose(
                                p[:, j * 128:(j + 1) * 128],
                                xl2t[b][:, k * 128:(k + 1) * 128], ident[:]))
                        else:
                            rec("tensor", lambda e, b=b, k=kcc, p=pt, j=j: e.transpose(
                                p[:, j * 128:(j + 1) * 128],
                                xl2t[b][:, k * 128:(k + 1) * 128], ident[:]).then_inc(s_tp.h, 1))
                    vtp = s_tp.bump()
                    if grp == 1:
                        ck[("xl2t", b)] = vtp
                    rec.wait("scalar", s_tp, vtp)
                    if grp == 0:
                        rec.wait("scalar", s_x2st[b], ck.get(("xl2f2", b), 0))
                    dst = xl2f2[b][:].rearrange("p (k t) -> p k t", k=KC)[:, grp * 8:(grp + 1) * 8, :]
                    srcp = pt[:].rearrange("p (j t) -> p j t", j=8)
                    rec.op("scalar", lambda e, d=dst, s=srcp: e.activation(
                        d, s, AF.Identity), s_fm)
                    ck[("pt3", ti % 2)] = s_fm.bump()
                rec.wait("scalar", s_fm, ck[("pt3", (tp3[0] - 1) % 2)])
                ck[("xl2f2", b)] = rec.dma(
                    "scalar", s_x2st[b],
                    xl2_scr.rearrange("k p n -> p k n")[:, :, tce * 128:(tce + 1) * 128],
                    xl2f2[b][:].rearrange("p (k t) -> p k t", k=KC))
            for gidx in range(seq // 512):
                gbi = gidx % 2
                rec.wait("sync", s_ev3, ck.get(("ain", gbi), 0))
                va = rec.dma("sync", s_ain[gbi], ain[gbi][:].rearrange("p (k t) -> p k t", k=KC),
                             attn_scr.rearrange("k p n -> p k n")[:, :, gidx * 512:(gidx + 1) * 512])
                ain3 = ain[gbi][:].rearrange("p (k t) -> p k t", k=KC)
                for tcl in range(4):
                    tc = gidx * 4 + tcl
                    bi = tc % 2
                    Q0 = qsum2[:, 4 * bi + 0:4 * bi + 1]
                    Q1 = qsum2[:, 4 * bi + 1:4 * bi + 2]
                    Q2 = qsum2[:, 4 * bi + 2:4 * bi + 3]
                    Q3 = qsum2[:, 4 * bi + 3:4 * bi + 4]
                    SS3 = sm3[:, 6 * bi + 0:6 * bi + 1]
                    M3 = sm3[:, 6 * bi + 1:6 * bi + 2]
                    V3 = sm3[:, 6 * bi + 2:6 * bi + 3]
                    L3 = sm3[:, 6 * bi + 3:6 * bi + 4]
                    RS3 = sm3[:, 6 * bi + 4:6 * bi + 5]
                    NM3 = sm3[:, 6 * bi + 5:6 * bi + 6]
                    rec.wait("gpsimd", s_x1, ck.get(("xb3", bi), 0))
                    vx = rec.dma("gpsimd", s_xld[bi], xb3[bi][:], xv[tc])
                    rec.wait("tensor", s_ain[gbi], va)
                    if tc == 0:
                        rec.wait("tensor", s_wol, vwo)
                    for qd in range(4):
                        qi = nq3[0]
                        nq3[0] += 1
                        pqb = p_q3[qi % 6]
                        rec.wait("tensor", s_ev3, ck.get(("pq3", qi % 6), 0))
                        for kcc in range(KC):
                            l_ap = ain3[:, kcc, tcl * 128:(tcl + 1) * 128]
                            r_ap = wot3[:, kcc, qd * 512:(qd + 1) * 512]
                            stf = kcc == 0
                            if kcc == KC - 1:
                                rec("tensor", lambda e, o=pqb, l=l_ap, r=r_ap, s=stf: e.matmul(
                                    o[:], l, r, start=s, stop=True).then_inc(s_wo3.h, 1))
                            else:
                                rec("tensor", lambda e, o=pqb, l=l_ap, r=r_ap, s=stf: e.matmul(
                                    o[:], l, r, start=s, stop=False))
                        vq3 = s_wo3.bump()
                        rec.wait("scalar", s_wo3, vq3)
                        if qd == 0:
                            rec.wait("scalar", s_xl2, ck.get(("p3adve", bi), 0))
                        rec.op("scalar", lambda e, p=pqb, b=bi, qd=qd: e.activation(
                            wsb3[b][:, qd * 512:(qd + 1) * 512], p[:], AF.Identity,
                            accum_out=qsum2[:, 4 * bi + qd:4 * bi + qd + 1]), s_ev3)
                        ck[("pq3", qi % 6)] = s_ev3.bump()
                    vev = s_ev3.v
                    rec.wait("scalar", s_ev3, vev)
                    if debug:
                        rec.dma("scalar", s_x1st[bi], dbg_wo[tc * 128:(tc + 1) * 128, :], wsb3[bi][:])
                    rec.op("scalar", lambda e, b=bi: e.activation(
                        sq3[:], wsb3[b][:], AF.Square, accum_out=SS3), s_sq)
                    vsq = s_sq.bump()
                    rec.wait("vector", s_sq, vsq)
                    rec.op("vector", lambda e: e.tensor_add(Q0, Q0, Q1))
                    rec.op("vector", lambda e: e.tensor_add(Q2, Q2, Q3))
                    rec.op("vector", lambda e: e.tensor_add(Q0, Q0, Q2))
                    rec.op("vector", lambda e: e.tensor_single_scalar(M3, Q0, 1.0 / D, ALU.mult))
                    rec.op("vector", lambda e: e.tensor_single_scalar(V3, SS3, 1.0 / D, ALU.mult))
                    rec.op("vector", lambda e: e.tensor_mul(Q1, M3, M3))
                    rec.op("vector", lambda e: e.tensor_sub(V3, V3, Q1))
                    rec.op("vector", lambda e: e.tensor_single_scalar(
                        V3, V3, LN_EPS, ALU.add), s_var)
                    vv3 = s_var.bump()
                    rec.wait("scalar", s_var, vv3)
                    rec.op("scalar", lambda e: e.activation(L3, V3, AF.Ln))
                    rec.op("scalar", lambda e: e.activation(
                        RS3, L3, AF.Exp, scale=-0.5), s_rstd)
                    vr3 = s_rstd.bump()
                    rec.wait("vector", s_rstd, vr3)
                    rec.wait("vector", s_xld[bi], vx)
                    rec.wait("vector", s_x1st[bi], ck.get(("x1b", bi), 0))
                    rec.op("vector", lambda e: e.scalar_tensor_tensor(
                        NM3, M3, -1.0, RS3, ALU.mult, ALU.mult))
                    rec.op("vector", lambda e, b=bi: e.tensor_scalar(
                        lnb[:], wsb3[b][:], RS3, NM3, ALU.mult, ALU.add))
                    rec.op("vector", lambda e, b=bi: e.tensor_add(
                        x1b[b][:], lnb[:], xb3[b][:]), s_x1)
                    ck[("xb3", bi)] = s_x1.bump()
                    rec.wait("scalar", s_x1, ck[("xb3", bi)])
                    ck[("x1b", bi)] = rec.dma("scalar", s_x1st[bi],
                                              x1_scr[tc * 128:(tc + 1) * 128, :], x1b[bi][:])
                    rec.op("scalar", lambda e, b=bi: e.activation(
                        sq3[:], x1b[b][:], AF.Square, accum_out=SS3), s_sq)
                    vsq2 = s_sq.bump()
                    rec.wait("vector", s_sq, vsq2)
                    rec.op("vector", lambda e, b=bi: e.reduce_sum(Q0, x1b[b][:], axis=AX.X))
                    rec.op("vector", lambda e: e.tensor_single_scalar(M3, Q0, 1.0 / D, ALU.mult))
                    rec.op("vector", lambda e: e.tensor_single_scalar(V3, SS3, 1.0 / D, ALU.mult))
                    rec.op("vector", lambda e: e.tensor_mul(Q1, M3, M3))
                    rec.op("vector", lambda e: e.tensor_sub(V3, V3, Q1))
                    rec.op("vector", lambda e: e.tensor_single_scalar(
                        V3, V3, LN_EPS, ALU.add), s_var)
                    vv4 = s_var.bump()
                    rec.wait("scalar", s_var, vv4)
                    rec.op("scalar", lambda e: e.activation(L3, V3, AF.Ln))
                    rec.op("scalar", lambda e: e.activation(
                        RS3, L3, AF.Exp, scale=-0.5), s_rstd)
                    vr4 = s_rstd.bump()
                    rec.wait("vector", s_rstd, vr4)
                    rec.wait("vector", s_tp, ck.get(("xl2t", bi), 0))
                    rec.op("vector", lambda e: e.scalar_tensor_tensor(
                        NM3, M3, -1.0, RS3, ALU.mult, ALU.mult))
                    rec.op("vector", lambda e, b=bi: e.tensor_scalar(
                        xl2t[b][:], x1b[b][:], RS3, NM3, ALU.mult, ALU.add), s_xl2)
                    ck[("p3adve", bi)] = s_xl2.bump()
                    ck[("xl2done", tc)] = ck[("p3adve", bi)]
                    if tc >= 1:
                        emit_tp3(tc - 1)
                    if tc == NCH - 1:
                        emit_tp3(tc)
                ck[("ain", gbi)] = s_ev3.v

            rec.wait("scalar", s_x1st[0], s_x1st[0].v)
            rec.wait("scalar", s_x1st[1], s_x1st[1].v)
            rec.wait("scalar", s_x2st[0], s_x2st[0].v)
            rec.wait("scalar", s_x2st[1], s_x2st[1].v)
            rec.wait("sync", s_ain[0], s_ain[0].v)
            rec.wait("sync", s_ain[1], s_ain[1].v)
            rec.wait("gpsimd", s_xld[0], s_xld[0].v)
            rec.wait("gpsimd", s_xld[1], s_xld[1].v)
            rec.wait("gpsimd", s_wol, s_wol.v)
            rec.flush()

        with ExitStack() as c5:
            rec = Rec(nc, chain, last_op)
            dgb = sb("dgb", [128, FNC * 512], BF16, c5)
            xt3b = [sb(f"xt3{i}", [128, KC * 512], BF16, c5) for i in range(2)]
            w0bu = [sb(f"w0bu{i}", [128, KC * 256], BF16, c5) for i in range(2)]
            w1bu = [sb(f"w1bu{i}", [128, 8 * 512], BF16, c5) for i in range(2)]
            x1c = [sb(f"x1c{i}", [128, D], F32, c5) for i in range(2)]
            esb = [sb(f"esb{i}", [128, D], F32, c5) for i in range(4)]
            yb = [sb(f"yb{i}", [128, D], F32, c5) for i in range(2)]
            sq5 = yb
            epart = sb("epart", [128, 16], F32, c5)
            sm_m = sb("sm_m", [128, 4], F32, c5)
            sm_v = sb("sm_v", [128, 4], F32, c5)
            sm_y = sb("sm_y", [128, 4], F32, c5)
            sm_z = sb("sm_z", [128, 4], F32, c5)
            sm_i = sb("sm_i", [128, 4], I32, c5)
            sm_n = sb("sm_n", [128, 4], F32, c5)
            p_f1 = [pst(f"p_f1{i}", [128, 512], F32, c5) for i in range(3)]
            p_f2 = [pst(f"p_f2{i}", [128, 512], F32, c5) for i in range(4)]

            w0v = w0b_scr.rearrange("(k p) n -> p k n", p=128)
            w1v = w1b_scr.rearrange("(c p) n -> p c n", p=128)
            x1v = x1_scr.rearrange("(c p) d -> c p d", p=128)
            yv = y_out.rearrange("(c p) d -> c p d", p=128)
            nf1 = [0]
            nw0 = [0]
            nw1 = [0]

            for t in range(seq // 512):
                xti = t % 2
                rec.wait("sync", s_ge, ck.get(("xt3", xti), 0))
                vxt = rec.dma("sync", s_xtl[xti], xt3b[xti][:].rearrange("p (k t) -> p k t", k=KC),
                              xl2_scr.rearrange("k p n -> p k n")[:, :, t * 512:(t + 1) * 512])
                vx1_t = {}
                for tc in range(2):
                    tcg = t * 4 + tc
                    xbi = tcg % 2
                    rec.wait("gpsimd", s_y, ck.get(("x1c", xbi), 0))
                    vx1_t[tc] = (xbi, rec.dma("gpsimd", s_x1l[xbi], x1c[xbi][:], x1v[tcg]))
                xt33 = xt3b[xti][:].rearrange("p (k t) -> p k t", k=KC)
                for blk in range(FF // 256):
                    wbi = nw0[0] % 2
                    nw0[0] += 1
                    rec.wait("sync", s_ge, ck.get(("w0bu", wbi), 0))
                    vw0 = rec.dma("sync", s_w0l[wbi],
                                  w0bu[wbi][:].rearrange("p (k n) -> p k n", k=KC),
                                  w0v[:, :, blk * 256:(blk + 1) * 256])
                    rec.wait("tensor", s_w0l[wbi], vw0)
                    if blk == 0:
                        rec.wait("tensor", s_xtl[xti], vxt)
                    w03 = w0bu[wbi][:].rearrange("p (k n) -> p k n", k=KC)
                    for dnc in range(2):
                        ncg = blk * 2 + dnc
                        fi = nf1[0]
                        nf1[0] += 1
                        pb = p_f1[fi % 3]
                        rec.wait("tensor", s_ge, ck.get(("pf1", fi % 3), 0))
                        for kcc in range(KC):
                            l_ap = w03[:, kcc, dnc * 128:(dnc + 1) * 128]
                            r_ap = xt33[:, kcc, :]
                            stf = kcc == 0
                            if kcc == KC - 1:
                                rec("tensor", lambda e, o=pb, l=l_ap, r=r_ap, s=stf: e.matmul(
                                    o[:], l, r, start=s, stop=True).then_inc(s_f1.h, 1))
                            else:
                                rec("tensor", lambda e, o=pb, l=l_ap, r=r_ap, s=stf: e.matmul(
                                    o[:], l, r, start=s, stop=False))
                        vf1 = s_f1.bump()
                        rec.wait("scalar", s_f1, vf1)
                        if ncg == 0 and t >= 1:
                            rec.wait("scalar", s_ev2, ck["dgdone"])
                        rec.op("scalar", lambda e, p=pb, n=ncg: e.activation(
                            dgb[:, n * 512:(n + 1) * 512], p[:], AF.Gelu,
                            bias=b0_sb[:, n:n + 1]), s_ge)
                        ck[("pf1", fi % 3)] = s_ge.bump()
                    ck[("w0bu", wbi)] = s_ge.v
                ck[("xt3", xti)] = s_ge.v
                vge_tile = s_ge.v
                if debug and t == 0:
                    rec.wait("gpsimd", s_ge, vge_tile)
                    rec.dma("gpsimd", s_cv[0], dbg_dg,
                            dgb[:].rearrange("p (c n) -> p c n", c=FNC))

                for db in range(4):
                    for sub in range(8):
                        wbi = nw1[0] % 2
                        nw1[0] += 1
                        g1 = ck.get(("w1bu", wbi), None)
                        if g1 is not None:
                            gsem, gval = g1
                            rec.wait("sync", s_f2s if gsem == "f2s" else s_ev2, gval)
                        vw1 = rec.dma("sync", s_w1l[wbi],
                                      w1bu[wbi][:].rearrange("p (c n) -> p c n", c=8),
                                      w1v[:, sub * 8:(sub + 1) * 8, db * 512:(db + 1) * 512])
                        rec.wait("tensor", s_w1l[wbi], vw1)
                        if db == 0 and sub == 0:
                            rec.wait("tensor", s_ge, vge_tile)
                        w13 = w1bu[wbi][:].rearrange("p (c n) -> p c n", c=8)
                        for tc in range(4):
                            pb = p_f2[tc]
                            if sub == 0:
                                rec.wait("tensor", s_ev2, ck.get(("pf2", tc), 0))
                            for c8 in range(8):
                                ncg = sub * 8 + c8
                                l_ap = dgb[:, ncg * 512 + tc * 128: ncg * 512 + (tc + 1) * 128]
                                r_ap = w13[:, c8, :]
                                stf = (sub == 0) and (c8 == 0)
                                spf = (sub == 7) and (c8 == 7)
                                lastmm = (tc == 3) and (c8 == 7) and (sub < 7)
                                if spf:
                                    rec("tensor", lambda e, o=pb, l=l_ap, r=r_ap, s=stf: e.matmul(
                                        o[:], l, r, start=s, stop=True).then_inc(s_f2.h, 1))
                                elif lastmm:
                                    rec("tensor", lambda e, o=pb, l=l_ap, r=r_ap, s=stf: e.matmul(
                                        o[:], l, r, start=s, stop=False).then_inc(s_f2s.h, 1))
                                else:
                                    rec("tensor", lambda e, o=pb, l=l_ap, r=r_ap, s=stf: e.matmul(
                                        o[:], l, r, start=s, stop=False))
                            if sub == 7:
                                s_f2.bump()
                        if sub < 7:
                            ck[("w1bu", wbi)] = ("f2s", s_f2s.bump())
                        else:
                            ck[("w1bu", wbi)] = ("pend", wbi)
                    for tc in range(4):
                        vstop = s_f2.v
                        rec.wait("vector", s_f2, vstop)
                        if db == 0:
                            rec.wait("vector", s_sq, ck.get(("esb", tc), 0))
                        rec.op("vector", lambda e, p=p_f2[tc], tc=tc, db=db: e.scalar_tensor_tensor(
                            esb[tc][:, db * 512:(db + 1) * 512], p[:], 1.0,
                            b1b[:, db * 512:(db + 1) * 512], ALU.mult, ALU.add,
                            accum_out=epart[:, 4 * tc + db:4 * tc + db + 1]), s_ev2)
                        ck[("pf2", tc)] = s_ev2.bump()
                    for key in list(ck):
                        if isinstance(ck.get(key), tuple) and ck[key][0] == "pend":
                            ck[key] = ("ev2", s_ev2.v)
                ck["dgdone"] = s_ev2.v

                for tc in range(4):
                    tcg = t * 4 + tc
                    ybi0 = tcg % 2
                    rec.wait("scalar", s_ev2, ck[("pf2", tc)])
                    if debug:
                        rec.dma("scalar", s_yst[tcg % 2], dbg_e[tcg * 128:(tcg + 1) * 128, :], esb[tc][:])
                    rec.wait("scalar", s_y, ck.get(("sq5", tc), 0))
                    rec.wait("scalar", s_yst[ybi0], ck.get(("yb", ybi0), 0))
                    rec.op("scalar", lambda e, tc=tc, b=ybi0: e.activation(
                        sq5[b][:], esb[tc][:], AF.Square, accum_out=sm_v[:, tc:tc + 1]), s_sq)
                    vsq5 = s_sq.bump()
                    ck[("esb", tc)] = vsq5
                ep3 = epart[:].rearrange("p (t d) -> p t d", t=4)
                rec.wait("vector", s_sq, s_sq.v)
                rec.op("vector", lambda e: e.tensor_add(sm_z[:], ep3[:, :, 0], ep3[:, :, 1]))
                rec.op("vector", lambda e: e.tensor_add(sm_m[:], ep3[:, :, 2], ep3[:, :, 3]))
                rec.op("vector", lambda e: e.tensor_add(sm_m[:], sm_z[:], sm_m[:]))
                rec.op("vector", lambda e: e.tensor_single_scalar(sm_m[:], sm_m[:], 1.0 / D, ALU.mult))
                rec.op("vector", lambda e: e.tensor_single_scalar(sm_v[:], sm_v[:], 1.0 / D, ALU.mult))
                rec.op("vector", lambda e: e.tensor_mul(sm_z[:], sm_m[:], sm_m[:]))
                rec.op("vector", lambda e: e.tensor_sub(sm_v[:], sm_v[:], sm_z[:]))
                rec.op("vector", lambda e: e.tensor_single_scalar(sm_v[:], sm_v[:], LN_EPS, ALU.add))
                rec.op("vector", lambda e: e.tensor_single_scalar(
                    sm_i[:], sm_v[:].bitcast(I32), 1, ALU.arith_shift_right))
                rec.op("vector", lambda e: e.tensor_scalar(sm_i[:], sm_i[:], -1, MAGIC, ALU.mult, ALU.add))
                rec.op("vector", lambda e: e.tensor_copy(sm_y[:], sm_i[:].bitcast(F32)))
                for _ in range(3):
                    rec.op("vector", lambda e: e.tensor_mul(sm_z[:], sm_y[:], sm_y[:]))
                    rec.op("vector", lambda e: e.tensor_mul(sm_z[:], sm_z[:], sm_v[:]))
                    rec.op("vector", lambda e: e.tensor_scalar(sm_z[:], sm_z[:], -0.5, 1.5, ALU.mult, ALU.add))
                    rec.op("vector", lambda e: e.tensor_mul(sm_y[:], sm_y[:], sm_z[:]))
                rec.op("vector", lambda e: e.scalar_tensor_tensor(
                    sm_n[:], sm_m[:], -1.0, sm_y[:], ALU.mult, ALU.mult))
                for tc in range(4):
                    tcg = t * 4 + tc
                    xbi, vx1 = vx1_t[tc]
                    t0 = slice(tc, tc + 1)
                    ybi = tcg % 2
                    rec.wait("vector", s_x1l[xbi], vx1)
                    rec.wait("vector", s_yst[ybi], ck.get(("yb", ybi), 0))
                    rec.op("vector", lambda e, tc=tc, o=t0, b=ybi: e.tensor_scalar(
                        yb[b][:], esb[tc][:], sm_y[:, o], sm_n[:, o], ALU.mult, ALU.add))
                    rec.op("vector", lambda e, b=ybi, bx=xbi: e.tensor_add(
                        yb[b][:], yb[b][:], x1c[bx][:]), s_y)
                    vy = s_y.bump()
                    ck[("x1c", xbi)] = vy
                    ck[("sq5", tc)] = vy
                    rec.wait("scalar", s_y, vy)
                    ck[("yb", ybi)] = rec.dma("scalar", s_yst[ybi], yv[tcg], yb[ybi][:])
                    if tc in (0, 1):
                        tcgn = t * 4 + tc + 2
                        xbin = tcgn % 2
                        rec.wait("gpsimd", s_y, ck.get(("x1c", xbin), 0))
                        vx1_t[tc + 2] = (xbin, rec.dma("gpsimd", s_x1l[xbin], x1c[xbin][:], x1v[tcgn]))

            rec.wait("scalar", s_yst[0], s_yst[0].v)
            rec.wait("scalar", s_yst[1], s_yst[1].v)
            rec.wait("sync", s_xtl[0], s_xtl[0].v)
            rec.wait("sync", s_xtl[1], s_xtl[1].v)
            rec.wait("sync", s_w0l[0], s_w0l[0].v)
            rec.wait("sync", s_w0l[1], s_w0l[1].v)
            rec.wait("sync", s_w1l[0], s_w1l[0].v)
            rec.wait("sync", s_w1l[1], s_w1l[1].v)
            rec.wait("gpsimd", s_x1l[0], s_x1l[0].v)
            rec.wait("gpsimd", s_x1l[1], s_x1l[1].v)
            rec.flush()

    return nc


_NC_CACHE = {}


def _get_program(seq):
    if seq not in _NC_CACHE:
        _NC_CACHE[seq] = build_program(seq)
    return _NC_CACHE[seq]


def kernel(L0_attn_x, L0_attn_Wq, L0_attn_Wk, L0_attn_Wv, L0_attn_Wo,
           L0_ffn_W0, L0_ffn_bias0, L0_ffn_W1, L0_ffn_bias1):
    x = np.ascontiguousarray(np.asarray(L0_attn_x, dtype=np.float32))
    b, seq, d = x.shape
    assert d == D
    tr = lambda w: np.ascontiguousarray(np.asarray(w, dtype=np.float32).T)
    weights = {
        "wqT": tr(L0_attn_Wq), "wkT": tr(L0_attn_Wk), "wvT": tr(L0_attn_Wv),
        "woT": tr(L0_attn_Wo), "w0T": tr(L0_ffn_W0), "w1T": tr(L0_ffn_W1),
        "b0": np.ascontiguousarray(np.asarray(L0_ffn_bias0, dtype=np.float32)),
        "b1": np.ascontiguousarray(np.asarray(L0_ffn_bias1, dtype=np.float32)),
    }
    nc = _get_program(seq)
    in_maps = [dict(weights, x=np.ascontiguousarray(x[i])) for i in range(b)]
    res = run_bass_kernel_spmd(nc, in_maps, core_ids=list(range(b)))
    out = np.stack([np.asarray(res.results[i]["y"]) for i in range(b)], axis=0)
    return out.astype(np.float32)
